# version 04
# baseline (speedup 1.0000x reference)
import dataclasses
import numpy as np
import concourse.bass as bass
import concourse.mybir as mybir
from concourse.bass_utils import run_bass_kernel_spmd

F32 = mybir.dt.float32
BF16 = mybir.dt.bfloat16
AF = mybir.ActivationFunctionType
ALU = mybir.AluOpType
AX = mybir.AxisListType

ENGINES = ("sync", "tensor", "vector", "scalar", "gpsimd")
TRUST_SAME_ENGINE = {"tensor"}

S_LEN = 2048
D = 1024
NH = 16
DK = 64
NCHUNK = 8
NTC = 16
NNT = 4
VST = 66
T_VSBT = 3968
A_VSBT = 127


class Prog:
    def __init__(self, nc):
        self.nc = nc
        self.ops = []
        self.buf = {}
        self.bar = {e: -1 for e in ENGINES}

    def _st(self, b):
        if b not in self.buf:
            self.buf[b] = {"writer": None, "readers": []}
        return self.buf[b]

    def op(self, engine, emit, reads=(), writes=(), sreads=(), dma=False, tag="",
           nobarrier=False, sem=None):
        idx = len(self.ops)
        deps = set()
        sdeps = set()
        for b in reads:
            st = self._st(b)
            if st["writer"] is not None:
                deps.add(st["writer"])
        for b in sreads:
            st = self._st(b)
            if st["writer"] is not None:
                deps.add(st["writer"])
                sdeps.add(st["writer"])
        for b in writes:
            st = self._st(b)
            if st["writer"] is not None:
                deps.add(st["writer"])
            deps.update(st["readers"])
        for e, last in self.bar.items():
            if last >= 0:
                deps.add(last)
        self.ops.append(dict(idx=idx, engine=engine, emit=emit, deps=deps,
                             sdeps=sdeps, dma=dma, tag=tag, sem=(sem or engine)))
        for b in reads:
            self._st(b)["readers"].append(idx)
        for b in sreads:
            self._st(b)["readers"].append(idx)
        for b in writes:
            self.buf[b] = {"writer": idx, "readers": []}
        if not nobarrier:
            self._last_nb = getattr(self, "_last_nb", {e: -1 for e in ENGINES})
            self._last_nb[engine] = idx
        return idx

    def barrier(self):
        last = getattr(self, "_last_nb", {e: -1 for e in ENGINES})
        self.bar = dict(last)

    def emit(self):
        nc = self.nc
        n = len(self.ops)

        def will_wait(d, o):
            de = self.ops[d]["engine"]
            if de != o["engine"]:
                return True
            if de in TRUST_SAME_ENGINE and d not in o["sdeps"]:
                return False
            return True

        need_inc = [False] * n
        for o in self.ops:
            if o["dma"]:
                need_inc[o["idx"]] = True
            for d in o["deps"]:
                if will_wait(d, o):
                    need_inc[d] = True
        semnames = []
        for o in self.ops:
            if o["sem"] not in semnames:
                semnames.append(o["sem"])
        count_after = {}
        totals = {sn: 0 for sn in semnames}
        for o in self.ops:
            if need_inc[o["idx"]]:
                totals[o["sem"]] += 16 if o["dma"] else 1
            count_after[o["idx"]] = totals[o["sem"]]
        waits = []
        for o in self.ops:
            w = {}
            for d in o["deps"]:
                if not will_wait(d, o):
                    continue
                dsn = self.ops[d]["sem"]
                c = count_after[d]
                if c > 0:
                    w[dsn] = max(w.get(dsn, 0), c)
            waits.append(w)
        low = {e: {sn: 0 for sn in semnames} for e in ENGINES}
        pruned = []
        for o in self.ops:
            e = o["engine"]
            w2 = []
            for se, v in sorted(waits[o["idx"]].items()):
                if low[e][se] < v:
                    w2.append((se, v))
                    low[e][se] = v
            pruned.append(w2)

        sems = {sn: nc.alloc_semaphore(f"sem_{sn}") for sn in semnames}
        per_engine = {e: [] for e in ENGINES}
        for o in self.ops:
            per_engine[o["engine"]].append(o)

        def runner(eng_name):
            def run(eng):
                for o in per_engine[eng_name]:
                    for (se, v) in pruned[o["idx"]]:
                        eng.wait_ge(sems[se], v)
                    ins = o["emit"]()
                    if need_inc[o["idx"]]:
                        amt = 16 if o["dma"] else 1
                        ins.then_inc(sems[o["sem"]], amt)
                if eng_name == "sync":
                    for sn in semnames:
                        if totals[sn] > 0:
                            eng.wait_ge(sems[sn], totals[sn])
            return run

        with nc.Block() as block:
            block.sync(runner("sync"))
            block.tensor(runner("tensor"))
            block.vector(runner("vector"))
            block.scalar(runner("scalar"))
            block.gpsimd(runner("gpsimd"))


def _bucket_table():
    nb = 16
    max_exact = 8
    delta = np.arange(-2047, 2048, dtype=np.int64)
    buckets = (delta > 0).astype(np.int64) * nb
    rp = np.abs(delta)
    is_small = rp < max_exact
    log_ratio = (np.log(np.maximum(rp, 1).astype(np.float32) / max_exact)
                 / np.float32(np.log(128.0 / max_exact)))
    large = max_exact + (log_ratio * (nb - max_exact)).astype(np.int32)
    large = np.minimum(large, nb - 1)
    return (buckets + np.where(is_small, rp, large)).astype(np.int64)


def build_program():
    nc = bass.Bass("TRN2", target_bir_lowering=False)
    din = lambda name, shape, dt=F32: nc.dram_tensor(name, shape, dt, kind="ExternalInput").ap()

    x_d = din("x", [2, S_LEN, D])
    wq_d = din("wq", [D, D]);  wk_d = din("wk", [D, D])
    wv_d = din("wv", [D, D]);  wo_d = din("wo", [D, D])
    wvf_d = din("wvf", [D, D])
    wgp_d = din("wgp", [D, 72])
    w1t_d = din("w1t", [D, 256]); w2t_d = din("w2t", [256, D])
    b2m_d = din("b2m", [8, D])
    bvr_d = din("bvr", [128, 8]); b1r_d = din("b1r", [128, 2])
    gp_d = din("gp", [1, 48])
    relb_d = din("relb", [32, 16])
    ohrev_d = din("ohrev", [32, 4095], BF16)
    identb_d = din("identb", [128, 128], BF16)
    identf_d = din("identf", [128, 128])
    onesr_d = din("onesr", [1, 128])
    expand_d = din("expand", [8, 256], BF16)
    parexp_d = din("parexp", [128, 128], BF16)

    out_d = nc.dram_tensor("out", [2, S_LEN, D], F32, kind="ExternalOutput").ap()
    vsbt_d = nc.dram_tensor("vsbt_dram", [16, 128, T_VSBT], BF16, kind="Internal").ap()
    vf_d = nc.dram_tensor("vf_dram", [8, 128, 2048], BF16, kind="Internal").ap()

    P = Prog(nc)
    SB = lambda name, shape, dt: nc.alloc_sbuf_tensor(name, shape, dt)

    xnf = SB("xnf", [128, 16384], BF16)
    qctx = SB("qctx", [128, 16384], BF16)
    kreg = SB("kreg", [128, 16384], BF16)
    vtok = SB("vtok", [128, 16 * 1056], BF16)
    g1 = SB("g1", [128, 24576], BF16)
    g3 = SB("g3", [128, 8192], BF16)

    negsh = SB("negsh", [128, 16], F32)
    konst = SB("konst", [128, 8], F32)
    bv_s = SB("bv_s", [128, 8], F32)
    b1_s = SB("b1_s", [128, 2], F32)
    grb = SB("grb", [128, 8], F32); brb = SB("brb", [128, 8], F32)
    gcb = SB("gcb", [128, 8], F32); bcb = SB("bcb", [128, 8], F32)
    bgb = SB("bgb", [128, 8], F32)
    sigb = SB("sigb", [128, 1], F32)
    identb = SB("identb_s", [128, 128], BF16)
    identf = SB("identf_s", [128, 128], F32)
    onesr = SB("onesr_s", [1, 128], F32)
    expand = SB("expand_s", [8, 256], BF16)
    parexp = SB("parexp_s", [128, 128], BF16)
    relb = SB("relb_s", [32, 16], BF16)
    gp_s = SB("gp_s", [1, 48], F32)
    vstage = SB("vstage", [128, 1024], BF16)
    ctxst = SB("ctxst", [128, 520], BF16)
    silst = SB("silst", [128, 1024], BF16)
    sgat = SB("sgat", [128, 16], F32)
    gwork = SB("gwork", [128, 256], F32)
    eq1a = SB("eq1a", [128, 128], F32)
    eq2a = SB("eq2a", [128, 128], F32)
    dda = SB("dda", [128, 16], F32)
    w1a = SB("w1a", [128, 16], F32)
    w2a = SB("w2a", [128, 16], F32)
    lnw = SB("lnw", [128, 24], F32)
    zc = SB("zc", [128, 8], F32)
    m16 = SB("m16", [16, 16], F32)
    r16 = SB("r16", [1, 32], F32)
    web_s = SB("web_s", [128, 16], BF16)

    psall = nc.alloc_psum_tensor("psall", [128, 4096], F32)
    psap = psall.ap()
    psb = psap.bitcast(BF16)

    def bank(i, r0=0, r1=128, c0=0, w=512):
        return psall[r0:r1, i * 512 + c0:i * 512 + c0 + w]

    def wide(i, c0=0, w=1024):
        return psall[:, i * 1024 + c0:i * 1024 + c0 + w]

    def bbank(i, r0=0, r1=128, c0=0, w=1024):
        return psb[r0:r1, i * 1024 + c0:i * 1024 + c0 + w]

    g1f = g1.ap().bitcast(F32)
    g3f = g3.ap().bitcast(F32)
    kf = kreg.ap().bitcast(F32)

    KV = {}
    def kcol(v):
        if v not in KV:
            i = len(KV)
            KV[v] = i
            P.op("vector", lambda v=v, i=i: nc.vector.memset(konst[:, i:i + 1], v),
                 writes=["konst"], tag=f"konst{i}")
        return konst[:, KV[v]:KV[v] + 1]

    def ld(engine, dst, src, key, tag):
        P.op(engine, lambda: getattr(nc, engine).dma_start(dst, src),
             writes=[key], dma=True, tag=tag, sem="s_" + key)

    ld("sync", identf[:, :], identf_d[:, :], "identf", "ld_identf")
    ld("sync", onesr[:, :], onesr_d[:, :], "onesr", "ld_onesr")
    ld("sync", gp_s[:, :], gp_d[:, :], "gp", "ld_gp")
    ld("sync", bv_s[:, :], bvr_d[:, :], "bv", "ld_bv")
    ld("sync", b1_s[:, :], b1r_d[:, :], "b1", "ld_b1")
    ld("sync", identb[:, :], identb_d[:, :], "identb", "ld_identb")
    ld("sync", expand[:, :], expand_d[:, :], "expand", "ld_expand")
    ld("sync", parexp[:, :], parexp_d[:, :], "parexp", "ld_parexp")
    ld("gpsimd", relb[:, :], relb_d[:, :], "relb", "ld_relb")
    ohrev_v = g1[0:32, 16384:16384 + 4095]
    ld("sync", ohrev_v, ohrev_d[:, :], "ohrev", "ld_ohrev")
    vrev_v = g1[0:16, 20480:20480 + 4095]

    kc_eps = kcol(1e-5); kc_8 = kcol(0.125); kc_nh = kcol(-0.5)
    kc_ph = kcol(0.5); kc_m1 = kcol(-1.0)

    P.op("tensor", lambda: nc.tensor.matmul(bank(6, c0=0, w=40)[:, 0:40], onesr[0:1, :],
                                            gp_s[0:1, 0:40], start=True, stop=True),
         reads=["onesr", "gp"], writes=["pb6"], tag="gpbcast")
    for t, key, c0 in [(grb, "grb", 0), (brb, "brb", 8), (gcb, "gcb", 16),
                       (bcb, "bcb", 24), (bgb, "bgb", 32)]:
        P.op("vector", lambda t=t, c0=c0: nc.vector.tensor_copy(t[:, :], bank(6)[:, c0:c0 + 8]),
             reads=["pb6"], writes=[key], tag=f"bc_{key}")
    siga = SB("siga", [1, 1], F32)
    P.op("scalar", lambda: nc.scalar.activation(siga[:, :], gp_s[0:1, 40:41], AF.Sigmoid),
         reads=["gp"], writes=["siga"], tag="sigalpha")
    P.op("tensor", lambda: nc.tensor.matmul(bank(7, c0=0, w=1)[:, 0:1], onesr[0:1, :],
                                            siga[0:1, :], start=True, stop=True),
         reads=["onesr", "siga"], writes=["pb7"], tag="sigbcast")
    P.op("vector", lambda: nc.vector.tensor_copy(sigb[:, :], bank(7)[:, 0:1]),
         reads=["pb7"], writes=["sigb"], tag="sigevac")

    for nt in range(8):
        w = 512 if nt < 7 else 511
        b = 4 + (nt % 2)
        P.op("tensor", lambda nt=nt, w=w, b=b: nc.tensor.matmul(
                bank(b, r1=16, w=w), relb[:, :], ohrev_v[:, nt * 512:nt * 512 + w],
                start=True, stop=True),
             reads=["relb", "ohrev"], writes=[f"pb{b}"], tag=f"vrev{nt}")
        P.op("scalar", lambda nt=nt, w=w, b=b: nc.scalar.activation(
                vrev_v[:, nt * 512:nt * 512 + w], bank(b, r1=16, w=w), AF.Exp),
             reads=[f"pb{b}"], writes=["vrev"], tag=f"vrevexp{nt}")
    for j in range(4):
        src = dataclasses.replace(
            g1.ap(), offset=20480 + A_VSBT - 32 * j,
            ap=[[24576, 16], [-1, 32], [1, T_VSBT]])
        P.op("gpsimd", lambda j=j, src=src: nc.gpsimd.dma_start(
                vsbt_d[:, 32 * j:32 * (j + 1), :], src),
             reads=["vrev"], writes=[f"vsbtd{j}"], dma=True, tag=f"vsbtbuild{j}",
             sem="vsbtb")

    WQ, WK, WV, WVF, WO = wq_d, wk_d, wv_d, wvf_d, wo_d

    def wslot(s):
        return g1[:, s * 8192:(s + 1) * 8192]

    def wload(slot, wdram, tag):
        dst = wslot(slot)
        for c in range(8):
            P.op("gpsimd", lambda c=c, dst=dst, wdram=wdram: nc.gpsimd.dma_start(
                    dst[:, c * 1024:(c + 1) * 1024],
                    wdram[c * 128:(c + 1) * 128, :]),
                 reads=([f"w{slot}c{c - 2}"] if c >= 2 else []),
                 writes=[f"w{slot}c{c}"], dma=True, tag=f"{tag}c{c}", nobarrier=True,
                 sem=f"wl{slot}{c % 2}")

    def vsbt_load(s, h, tag):
        slot = h % 2
        dst = g1[:, 16384 + slot * T_VSBT:16384 + (slot + 1) * T_VSBT]
        P.op("sync", lambda dst=dst, h=h: nc.sync.dma_start(dst, vsbt_d[h, :, :]),
             reads=["vsbtd0", "vsbtd1", "vsbtd2", "vsbtd3"],
             writes=[f"vsbt{slot}"], dma=True, tag=tag, nobarrier=True, sem=f"vsbt{slot}")

    wload(0, WQ, "ldwq_s0")

    for s in range(2):
        for tc in range(NTC):
            xin = g3f[:, (tc % 2) * 1024:(tc % 2) * 1024 + 1024]
            xkey = f"xin{tc % 2}"
            P.op("sync", lambda s=s, tc=tc, xin=xin: nc.sync.dma_start(
                    xin, x_d[s, tc * 128:(tc + 1) * 128, :]),
                 writes=[xkey], dma=True, tag=f"ldx{s}_{tc}", sem=f"xin{tc % 2}")
            lb = (tc % 2) * 5
            sums = lnw[:, lb:lb + 1]; ssq = lnw[:, lb + 1:lb + 2]; mean = lnw[:, lb + 2:lb + 3]
            var = lnw[:, lb + 3:lb + 4]; rstd = lnw[:, lb + 4:lb + 5]
            scr = g3f[:, 3072:4096]
            P.op("vector", lambda xin=xin, sums=sums: nc.vector.reduce_sum(sums, xin, AX.X),
                 reads=[xkey], writes=[f"ln_sums{tc % 2}"], tag=f"lnsum{tc}")
            P.op("vector", lambda xin=xin, scr=scr: nc.vector.scalar_tensor_tensor(
                    scr, xin, 0.0, xin, ALU.add, ALU.mult),
                 reads=[xkey], writes=["lnscr"], tag=f"lnsq{tc}")
            P.op("vector", lambda scr=scr, ssq=ssq: nc.vector.reduce_sum(ssq, scr, AX.X),
                 reads=["lnscr"], writes=[f"ln_ssq{tc % 2}"], tag=f"lnssq{tc}")
            P.op("vector", lambda sums=sums, mean=mean: nc.vector.tensor_scalar_mul(
                    mean, sums, 1.0 / 1024), reads=[f"ln_sums{tc % 2}"], writes=[f"ln_mean{tc % 2}"])
            P.op("vector", lambda mean=mean, var=var: nc.vector.scalar_tensor_tensor(
                    var, mean, -1.0, mean, ALU.mult, ALU.mult),
                 reads=[f"ln_mean{tc % 2}"], writes=[f"ln_var{tc % 2}"])
            P.op("vector", lambda ssq=ssq, var=var: nc.vector.scalar_tensor_tensor(
                    var, ssq, 1.0 / 1024, var, ALU.mult, ALU.add),
                 reads=[f"ln_ssq{tc % 2}", f"ln_var{tc % 2}"], writes=[f"ln_var{tc % 2}"])
            P.op("scalar", lambda var=var, rstd=rstd: nc.scalar.activation(
                    rstd, var, AF.Ln, bias=kc_eps), reads=[f"ln_var{tc % 2}", "konst"],
                 writes=[f"ln_rstd{tc % 2}"], tag=f"lnln{tc}")
            P.op("scalar", lambda rstd=rstd: nc.scalar.activation(
                    rstd, rstd, AF.Exp, scale=kc_nh),
                 sreads=[f"ln_rstd{tc % 2}"], reads=["konst"], writes=[f"ln_rstd{tc % 2}"], tag=f"lnexp{tc}")
            xnt = g3[:, 4096 + (tc % 2) * 1024:4096 + (tc % 2) * 1024 + 1024]
            xntk = f"xnt{tc % 2}"
            P.op("vector", lambda xin=xin, xnt=xnt, mean=mean, rstd=rstd:
                    nc.vector.tensor_scalar(xnt, xin, mean, rstd, ALU.subtract, ALU.mult),
                 reads=[xkey], sreads=[f"ln_mean{tc % 2}", f"ln_rstd{tc % 2}"], writes=[xntk],
                 tag=f"lnapply{tc}")
            tb = 6 + (tc % 2)
            for c in range(8):
                P.op("tensor", lambda c=c, tb=tb, xnt=xnt: nc.tensor.transpose(
                        bbank(tb, c0=c * 128, w=128), xnt[:, c * 128:(c + 1) * 128],
                        identb[:, :]),
                     reads=[xntk, "identb"], writes=[f"pb{tb}"], tag=f"lntr{tc}_{c}")
            dst = dataclasses.replace(xnf.ap(), offset=tc * 128,
                                      ap=[[16384, 128], [2048, 8], [1, 128]])
            P.op("vector", lambda dst=dst, tb=tb: nc.vector.tensor_copy(
                    dst, bbank(tb).rearrange("p (c m) -> p c m", c=8)),
                 reads=[f"pb{tb}"], writes=[f"xnf{tc}"], tag=f"lnevac{tc}")
        P.barrier()

        wload(1, WK, f"ldwk_s{s}")
        xnf_reads = [f"xnf{tc}" for tc in range(NTC)]

        def proj_feat(slot, dest, destname, evac_engine, tag):
            for mc in range(8):
                for nt in range(NNT):
                    b = (mc * NNT + nt) % 4
                    for kc in range(8):
                        P.op("tensor", lambda mc=mc, nt=nt, kc=kc, b=b, slot=slot:
                                nc.tensor.matmul(
                                    bank(b), wslot(slot)[:, kc * 1024 + mc * 128:kc * 1024 + (mc + 1) * 128],
                                    xnf[:, kc * 2048 + nt * 512:kc * 2048 + (nt + 1) * 512],
                                    start=(kc == 0), stop=(kc == 7)),
                             reads=[f"w{slot}c{kc}"] + xnf_reads, writes=[f"pb{b}"],
                             tag=f"{tag}mm{mc}_{nt}_{kc}")
                    if evac_engine == "scalar":
                        P.op("scalar", lambda mc=mc, nt=nt, b=b, dest=dest:
                                nc.scalar.activation(
                                    dest[:, mc * 2048 + nt * 512:mc * 2048 + (nt + 1) * 512],
                                    bank(b), AF.Copy),
                             reads=[f"pb{b}"], writes=[f"{destname}{mc}n{nt}"],
                             tag=f"{tag}ev{mc}_{nt}")
                    else:
                        P.op("vector", lambda mc=mc, nt=nt, b=b, dest=dest:
                                nc.vector.tensor_copy(
                                    dest[:, mc * 2048 + nt * 512:mc * 2048 + (nt + 1) * 512],
                                    bank(b)),
                             reads=[f"pb{b}"], writes=[f"{destname}{mc}n{nt}"],
                             tag=f"{tag}ev{mc}_{nt}")

        proj_feat(0, qctx, "q", "scalar", f"qp{s}")
        proj_feat(1, kreg, "k", "scalar", f"kp{s}")
        wload(0, WV, f"ldwv_s{s}")

        for tc in range(NTC):
            P.op("vector", lambda tc=tc: nc.vector.memset(
                    dataclasses.replace(vtok.ap(), offset=tc * 1056 + 64,
                                        ap=[[16896, 128], [VST, 16], [1, 1]]), 1.0),
                 writes=[f"vt{tc}"], tag=f"vones{tc}")
            for half in range(2):
                b = (tc * 2 + half) % 4
                for kc in range(8):
                    P.op("tensor", lambda tc=tc, half=half, kc=kc, b=b:
                            nc.tensor.matmul(
                                bank(b), xnf[:, kc * 2048 + tc * 128:kc * 2048 + (tc + 1) * 128],
                                wslot(0)[:, kc * 1024 + half * 512:kc * 1024 + (half + 1) * 512],
                                start=(kc == 0), stop=(kc == 7)),
                         reads=[f"w0c{kc}"] + xnf_reads, writes=[f"pb{b}"],
                         tag=f"vp{s}mm{tc}_{half}_{kc}")
                dst = dataclasses.replace(
                    vtok.ap(), offset=tc * 1056 + half * 8 * VST,
                    ap=[[16896, 128], [VST, 8], [1, 64]])
                P.op("vector", lambda dst=dst, b=b: nc.vector.tensor_copy(
                        dst, bank(b).rearrange("p (g x) -> p g x", g=8)),
                     reads=[f"pb{b}"], writes=[f"vt{tc}"], tag=f"vev{tc}_{half}")
        wload(1, WVF, f"ldwvf_s{s}")

        for mc in range(8):
            for nt in range(NNT):
                b = (mc * NNT + nt) % 4
                for kc in range(8):
                    P.op("tensor", lambda mc=mc, nt=nt, kc=kc, b=b: nc.tensor.matmul(
                            bank(b), wslot(1)[:, kc * 1024 + mc * 128:kc * 1024 + (mc + 1) * 128],
                            xnf[:, kc * 2048 + nt * 512:kc * 2048 + (nt + 1) * 512],
                            start=(kc == 0), stop=(kc == 7)),
                         reads=[f"w1c{kc}"] + xnf_reads, writes=[f"pb{b}"],
                         tag=f"vfmm{mc}_{nt}_{kc}")
                vs = vstage[:, (nt % 2) * 512:(nt % 2) * 512 + 512]
                P.op("vector", lambda mc=mc, b=b, vs=vs: nc.vector.tensor_scalar(
                        vs, bank(b), bv_s[:, mc:mc + 1], None, ALU.add),
                     reads=[f"pb{b}", "bv"], writes=[f"vstage{nt % 2}"],
                     tag=f"vfev{mc}_{nt}")
                P.op("sync", lambda mc=mc, nt=nt, vs=vs: nc.sync.dma_start(
                        vf_d[mc, :, nt * 512:(nt + 1) * 512], vs),
                     reads=[f"vstage{nt % 2}"], writes=[f"vfd{mc}n{nt}"], dma=True,
                     tag=f"vfst{mc}_{nt}", sem=f"vfst{nt % 2}")

        for side, reg, regk in ((0, qctx, "q"), (1, kreg, "k")):
            for c in range(8):
                sq = g3[:, (c % 2) * 2048:(c % 2) * 2048 + 2048]
                P.op("vector", lambda c=c, reg=reg, sq=sq: nc.vector.scalar_tensor_tensor(
                        sq, reg[:, c * 2048:(c + 1) * 2048], 0.0,
                        reg[:, c * 2048:(c + 1) * 2048], ALU.add, ALU.mult),
                     reads=[f"{regk}{c}n{j}" for j in range(4)],
                     writes=[f"sq{c % 2}"], tag=f"sq{side}_{c}")
                for nt in range(NNT):
                    qb = 4 + 2 * side + nt // 2
                    qbase = 64 * (nt % 2)
                    P.op("tensor", lambda c=c, nt=nt, sq=sq, qb=qb, qbase=qbase:
                            nc.tensor.matmul(
                                bank(qb, r0=qbase, r1=qbase + 16),
                                parexp[:, c * 16:(c + 1) * 16], sq[:, nt * 512:(nt + 1) * 512],
                                start=(c == 0 and nt % 2 == 0), stop=(c == 7),
                                tile_position=(0, qbase)),
                         reads=[f"sq{c % 2}"], writes=[f"pb{qb}"],
                         tag=f"q2mm{side}_{c}_{nt}")
            for nt in range(NNT):
                qb = 4 + 2 * side + nt // 2
                qbase = 64 * (nt % 2)
                P.op("vector", lambda nt=nt, side=side, qb=qb, qbase=qbase:
                        nc.vector.reduce_max(
                            m16[:, side * 8 + nt:side * 8 + nt + 1],
                            bank(qb, r0=qbase, r1=qbase + 16), AX.X),
                     reads=[f"pb{qb}"], writes=[f"m16_{side}"],
                     tag=f"q2red{side}_{nt}")

        P.op("vector", lambda: nc.vector.reduce_max(m16[0:16, 4:5], m16[0:16, 0:4], AX.X),
             reads=["m16_0"], writes=["mq2c"], tag="mq2c")
        P.op("vector", lambda: nc.vector.reduce_max(m16[0:16, 5:6], m16[0:16, 8:12], AX.X),
             reads=["m16_1"], writes=["mk2c"], tag="mk2c")
        P.op("vector", lambda: nc.vector.tensor_mul(m16[0:16, 6:7], m16[0:16, 4:5],
                                                    m16[0:16, 5:6]),
             reads=["mq2c", "mk2c"], writes=["prod2"], tag="prod2")
        P.op("scalar", lambda: nc.scalar.activation(m16[0:16, 7:8], m16[0:16, 6:7],
                                                    AF.Ln), reads=["prod2"],
             writes=["lnprod"], tag="lnprod")
        P.op("scalar", lambda: nc.scalar.activation(m16[0:16, 7:8], m16[0:16, 7:8],
                                                    AF.Exp, scale=kc_ph[0:16]),
             sreads=["lnprod"], reads=["konst"], writes=["lnprod"], tag="shifth")
        P.op("tensor", lambda: nc.tensor.transpose(bank(4, r1=1, w=16)[0:1, 0:16],
                                                   m16[0:16, 7:8], identf[0:16, 0:16]),
             reads=["lnprod", "identf"], writes=["pb4"], tag="shtr")
        P.op("vector", lambda: nc.vector.tensor_copy(r16[0:1, 0:16], bank(4, r1=1, w=16)[0:1, 0:16]),
             reads=["pb4"], writes=["r16"], tag="shrow")
        P.op("tensor", lambda: nc.tensor.matmul(bank(5, w=16)[:, 0:16], onesr[0:1, :],
                                                r16[0:1, 0:16], start=True, stop=True),
             reads=["onesr", "r16"], writes=["pb5"], tag="shbcast")
        P.op("vector", lambda: nc.vector.tensor_scalar_mul(negsh[:, :], bank(5, w=16)[:, 0:16],
                                                           -1.0),
             reads=["pb5"], writes=["negsh"], tag="negsh")
        vsbt_load(s, 0, f"vsbt{s}_0")
        vsbt_load(s, 1, f"vsbt{s}_1")
        P.barrier()

        for hp in range(8):
            h0, h1 = 2 * hp, 2 * hp + 1
            hc = hp
            quads_done = [[False] * 4, [False] * 4]

            def sc_quad(qt, quad):
                q0 = qt * 512
                tb = quad % 2
                for kc4 in range(4):
                    kc = quad * 4 + kc4
                    for hi, h in enumerate((h0, h1)):
                        base = (h % 2) * 64
                        sb = hi * 3 + kc % 3
                        P.op("tensor", lambda kc=kc, h=h, base=base, sb=sb, q0=q0, hc=hc:
                                nc.tensor.matmul(
                                    bank(sb),
                                    kreg[base:base + 64,
                                         hc * 2048 + kc * 128:hc * 2048 + (kc + 1) * 128],
                                    qctx[base:base + 64, hc * 2048 + q0:hc * 2048 + q0 + 512],
                                    start=True, stop=True, tile_position=(base, 0)),
                             reads=[f"k{hc}n{kc // 4}", f"q{hc}n{qt}"],
                             writes=[f"pb{sb}"],
                             tag=f"sc{h}_{qt}_{kc}")
                    for hi, h in enumerate((h0, h1)):
                        sb = hi * 3 + kc % 3
                        texp = g3[:, (hi * 2 + tb) * 2048:(hi * 2 + tb) * 2048 + 2048]
                        P.op("scalar", lambda sb=sb, h=h, kc4=kc4, texp=texp:
                                nc.scalar.activation(
                                    texp[:, kc4 * 512:(kc4 + 1) * 512], bank(sb),
                                    AF.Exp, bias=negsh[:, h:h + 1]),
                             reads=[f"pb{sb}", "negsh"],
                             writes=[f"texp{hi}b{tb}k{kc4}"], tag=f"exp{h}_{qt}_{kc}")
                for hi, h in enumerate((h0, h1)):
                    texp = g3[:, (hi * 2 + tb) * 2048:(hi * 2 + tb) * 2048 + 2048]
                    slot = h % 2
                    off = 16384 + slot * T_VSBT + (1920 - (quad * 4) * 128 + q0)
                    win = dataclasses.replace(g1.ap(), offset=off,
                                              ap=[[24576, 128], [-128, 4], [1, 512]])
                    wt = g1[:, hi * 8192 + quad * 2048:hi * 8192 + (quad + 1) * 2048]
                    P.op("vector", lambda texp=texp, win=win, wt=wt: nc.vector.tensor_tensor(
                            wt.rearrange("p (a b) -> p a b", a=4), texp.rearrange("p (a b) -> p a b", a=4),
                            win, ALU.mult),
                         reads=[f"texp{hi}b{tb}k{j}" for j in range(4)] + [f"vsbt{slot}"],
                         writes=[f"wt{hi}q{quad}"], tag=f"mult{h}_{qt}_{quad}")

            def ctx_group(qt, quad):
                for hi, h in enumerate((h0, h1)):
                    cb = 6 + hi
                    for qc in range(4):
                        for kc in range(quad * 4, quad * 4 + 4):
                            P.op("tensor", lambda hi=hi, h=h, qt=qt, qc=qc, kc=kc, cb=cb:
                                    nc.tensor.matmul(
                                        bank(cb, c0=qc * 65, w=65),
                                        g1[:, hi * 8192 + kc * 512 + qc * 128:
                                           hi * 8192 + kc * 512 + (qc + 1) * 128],
                                        vtok[:, kc * 1056 + h * VST:kc * 1056 + h * VST + 65],
                                        start=(qc == 0 and kc == 0), stop=(kc == 15)),
                                 reads=[f"wt{hi}q{kc // 4}", f"vt{kc}"],
                                 writes=[f"pb{cb}"], tag=f"ctx{h}_{qt}_{qc}_{kc}")

            def ctx_evac(qt):
                for hi, h in enumerate((h0, h1)):
                    cb = 6 + hi
                    zap = dataclasses.replace(psap, offset=cb * 512 + 64,
                                              ap=[[4096, 128], [65, 4], [1, 1]])
                    P.op("vector", lambda zap=zap, hi=hi: nc.vector.reciprocal(
                            zc[:, hi * 4:hi * 4 + 4], zap),
                         reads=[f"pb{cb}"], writes=[f"zc{hi}"], tag=f"zrec{h}_{qt}")
                    cst = ctxst[:, hi * 260:hi * 260 + 260]
                    for qc in range(4):
                        P.op("vector", lambda qc=qc, cb=cb, cst=cst, hi=hi:
                                nc.vector.tensor_scalar(
                                    cst[:, qc * 64:(qc + 1) * 64],
                                    bank(cb, c0=qc * 65, w=64),
                                    zc[:, hi * 4 + qc:hi * 4 + qc + 1], None, ALU.mult),
                             reads=[f"pb{cb}"], sreads=[f"zc{hi}"],
                             writes=[f"cst{hi}"], tag=f"ctxsc{h}_{qt}_{qc}")
                    tb = 6 + hi
                    for qc in range(4):
                        P.op("tensor", lambda qc=qc, cst=cst, tb=tb: nc.tensor.transpose(
                                bbank(tb, r1=64, c0=qc * 128, w=128),
                                cst[:, qc * 64:(qc + 1) * 64], identb[:, :]),
                             reads=[f"cst{hi}", "identb"], writes=[f"pb{tb}"],
                             tag=f"ctxtr{h}_{qt}_{qc}")
                    base = (h % 2) * 64
                    P.op("vector", lambda tb=tb, base=base, qt=qt, hc=hc: nc.vector.tensor_copy(
                            qctx[base:base + 64, hc * 2048 + qt * 512:hc * 2048 + (qt + 1) * 512],
                            bbank(tb, r1=64, w=512)[0:64, 0:512]),
                         reads=[f"pb{tb}"], writes=[f"q{hc}n{qt}"],
                         tag=f"ctxev{h}_{qt}")

            for quad in range(4):
                sc_quad(0, quad)
            for qt in range(4):
                for quad in range(4):
                    ctx_group(qt, quad)
                    if qt < 3:
                        sc_quad(qt + 1, quad)
                ctx_evac(qt)
            if hp < 7:
                vsbt_load(s, h0 + 2, f"vsbt{s}_{h0 + 2}")
                vsbt_load(s, h1 + 2, f"vsbt{s}_{h1 + 2}")
            elif s == 0:
                vsbt_load(1, 0, "vsbt1_0")
                vsbt_load(1, 1, "vsbt1_1")
        P.barrier()

        for mc in range(8):
            P.op("sync", lambda mc=mc: nc.sync.dma_start(
                    kreg[:, mc * 2048:(mc + 1) * 2048], vf_d[mc, :, :]),
                 reads=[f"vfd{mc}n{j}" for j in range(4)]
                       + ([f"vf{mc - 2}"] if mc >= 2 else []),
                 writes=[f"vf{mc}"], dma=True, tag=f"ldvf{mc}", sem=f"vfld{mc % 2}")
        for c in range(8):
            P.op("gpsimd", lambda c=c: nc.gpsimd.dma_start(
                    g3[:, c * 256:(c + 1) * 256], w1t_d[c * 128:(c + 1) * 128, :]),
                 writes=[f"w1t{c}", "w1tCH"], dma=True, tag=f"ldw1t{c}", sem="w1t")
        for c in range(2):
            P.op("gpsimd", lambda c=c: nc.gpsimd.dma_start(
                    g3[:, 2048 + c * 1024:2048 + (c + 1) * 1024],
                    w2t_d[c * 128:(c + 1) * 128, :]),
                 writes=[f"w2t{c}", "w2tCH"], dma=True, tag=f"ldw2t{c}", sem="w2t")
        for c in range(8):
            P.op("gpsimd", lambda c=c: nc.gpsimd.dma_start(
                    g3[:, 4096 + c * 72:4096 + (c + 1) * 72],
                    wgp_d[c * 128:(c + 1) * 128, :]),
                 writes=[f"wgp{c}", "wgpCH"], dma=True, tag=f"ldwgp{c}", sem="wgp")
        P.op("gpsimd", lambda: nc.gpsimd.dma_start(g3[0:8, 4672:4672 + 1024], b2m_d[:, :]),
             writes=["b2m"], dma=True, tag="ldb2m", sem="s_b2m")
        wload(1, WO, f"ldwo_s{s}")
        weT = g3[0:8, 5696:5696 + 2048]

        for tc in range(NTC):
            gb = 4 + (tc % 2)
            for kc in range(8):
                P.op("tensor", lambda tc=tc, kc=kc, gb=gb: nc.tensor.matmul(
                        bank(gb, w=72), kreg[:, kc * 2048 + tc * 128:kc * 2048 + (tc + 1) * 128],
                        g3[:, 4096 + kc * 72:4096 + (kc + 1) * 72],
                        start=(kc == 0), stop=(kc == 7)),
                     reads=[f"vf{kc}", f"wgp{kc}"], writes=[f"pb{gb}"],
                     tag=f"gmm{tc}_{kc}")
            gp_ = (tc % 2) * 128
            gev = gwork[:, gp_ + 0:gp_ + 72]
            P.op("vector", lambda gb=gb, gev=gev: nc.vector.tensor_copy(gev, bank(gb, w=72)),
                 reads=[f"pb{gb}"], writes=[f"g_ev{tc % 2}"], tag=f"gev{tc}")
            r_ = gwork[:, gp_ + 72:gp_ + 80]
            P.op("vector", lambda gev=gev, r_=r_: nc.vector.tensor_tensor(
                    r_, gev[:, 0:8], bgb[:, :], ALU.add),
                 reads=[f"g_ev{tc % 2}", "bgb"], writes=[f"g_r{tc % 2}"], tag=f"gr{tc}")
            pv2 = gev[:, 8:72]
            P.op("vector", lambda pv2=pv2: nc.vector.scalar_tensor_tensor(
                    pv2, pv2, 0.0, pv2, ALU.add, ALU.mult),
                 reads=[f"g_ev{tc % 2}"], writes=[f"g_ev{tc % 2}"], tag=f"gpv2{tc}")
            c2 = gwork[:, gp_ + 80:gp_ + 88]
            P.op("vector", lambda pv2=pv2, c2=c2: nc.vector.tensor_add(
                    c2, pv2[:, 0:8], pv2[:, 8:16]), reads=[f"g_ev{tc % 2}"], writes=[f"g_c2{tc % 2}"])
            for pstep in range(2, 8):
                P.op("vector", lambda pv2=pv2, c2=c2, pstep=pstep: nc.vector.tensor_add(
                        c2, c2, pv2[:, pstep * 8:(pstep + 1) * 8]),
                     reads=[f"g_ev{tc % 2}", f"g_c2{tc % 2}"], writes=[f"g_c2{tc % 2}"])
            def gln(srck, src, g, gk, b, bk, outk, out):
                glb = 10 + (tc % 2) * 3
                mean = lnw[:, glb:glb + 1]; ss2 = lnw[:, glb + 1:glb + 2]; rst = lnw[:, glb + 2:glb + 3]
                rc = gwork[:, gp_ + 0:gp_ + 8] if outk == f"g_lnr{tc % 2}" else gwork[:, gp_ + 16:gp_ + 24]
                scr = gwork[:, gp_ + 8:gp_ + 16]
                P.op("vector", lambda src=src, mean=mean: nc.vector.reduce_sum(mean, src, AX.X),
                     reads=[srck], writes=[f"g_mean{tc % 2}"])
                P.op("vector", lambda mean=mean: nc.vector.tensor_scalar_mul(mean, mean, 0.125),
                     reads=[f"g_mean{tc % 2}"], writes=[f"g_mean{tc % 2}"])
                P.op("vector", lambda src=src, rc=rc, mean=mean: nc.vector.tensor_scalar(
                        rc, src, mean, None, ALU.subtract),
                     reads=[srck], sreads=[f"g_mean{tc % 2}"], writes=[outk + "_rc", f"g_ev{tc % 2}"])
                P.op("vector", lambda rc=rc, scr=scr: nc.vector.scalar_tensor_tensor(
                        scr, rc, 0.0, rc, ALU.add, ALU.mult),
                     reads=[outk + "_rc"], writes=[f"g_scr{tc % 2}", f"g_ev{tc % 2}"])
                P.op("vector", lambda scr=scr, ss2=ss2: nc.vector.reduce_sum(ss2, scr, AX.X),
                     reads=[f"g_scr{tc % 2}"], writes=[f"g_ss2{tc % 2}"])
                P.op("scalar", lambda ss2=ss2, rst=rst: nc.scalar.activation(
                        rst, ss2, AF.Ln, bias=kc_eps, scale=kc_8),
                     reads=[f"g_ss2{tc % 2}", "konst"], writes=[f"g_rst{tc % 2}"])
                P.op("scalar", lambda rst=rst: nc.scalar.activation(
                        rst, rst, AF.Exp, scale=kc_nh),
                     sreads=[f"g_rst{tc % 2}"], reads=["konst"], writes=[f"g_rst{tc % 2}"])
                P.op("vector", lambda rc=rc, rst=rst, out=out: nc.vector.tensor_scalar(
                        out, rc, rst, None, ALU.mult),
                     reads=[outk + "_rc"], sreads=[f"g_rst{tc % 2}"], writes=[outk])
                P.op("vector", lambda out=out, g=g: nc.vector.tensor_mul(out, out, g[:, :]),
                     reads=[outk, gk], writes=[outk])
                P.op("vector", lambda out=out, b=b: nc.vector.tensor_add(out, out, b[:, :]),
                     reads=[outk, bk], writes=[outk])
            lnr = gwork[:, gp_ + 96:gp_ + 104]; lnc = gwork[:, gp_ + 104:gp_ + 112]
            gln(f"g_r{tc % 2}", r_, grb, "grb", brb, "brb", f"g_lnr{tc % 2}", lnr)
            cval = gwork[:, gp_ + 88:gp_ + 96]
            P.op("scalar", lambda c2=c2, cval=cval: nc.scalar.activation(
                    cval, c2, AF.Ln, scale=kc_8),
                 reads=[f"g_c2{tc % 2}", "konst"], writes=[f"g_cval{tc % 2}"], tag=f"gcln{tc}")
            P.op("scalar", lambda cval=cval: nc.scalar.activation(
                    cval, cval, AF.Exp, scale=kc_ph),
                 sreads=[f"g_cval{tc % 2}"], reads=["konst"], writes=[f"g_cval{tc % 2}"], tag=f"gcexp{tc}")
            gln(f"g_cval{tc % 2}", cval, gcb, "gcb", bcb, "bcb", f"g_lnc{tc % 2}", lnc)
            P.op("vector", lambda lnr=lnr, lnc=lnc: nc.vector.scalar_tensor_tensor(
                    sgat[:, (tc % 2) * 8:(tc % 2) * 8 + 8], lnc, sigb[:, :], lnr, ALU.mult, ALU.add),
                 reads=[f"g_lnr{tc % 2}", f"g_lnc{tc % 2}"], sreads=["sigb"], writes=[f"g_s{tc % 2}"],
                 tag=f"gcomb{tc}")
            mx8 = gwork[:, gp_ + 24:gp_ + 32]; mxp = gwork[:, gp_ + 32:gp_ + 40]
            P.op("vector", lambda mx8=mx8: nc.vector.max(out=mx8, in_=sgat[:, (tc % 2) * 8:(tc % 2) * 8 + 8]),
                 reads=[f"g_s{tc % 2}"], writes=[f"g_mx8{tc % 2}", f"g_ev{tc % 2}"], tag=f"gmax{tc}")
            P.op("vector", lambda mxp=mxp: nc.vector.memset(mxp, 1e38),
                 writes=[f"g_mxp{tc % 2}", f"g_ev{tc % 2}"])
            P.op("vector", lambda mxp=mxp, mx8=mx8: nc.vector.tensor_copy(mxp[:, 0:1], mx8[:, 0:1]),
                 reads=[f"g_mx8{tc % 2}"], writes=[f"g_mxp{tc % 2}"])
            srep = gwork[:, gp_ + 40:gp_ + 48]
            P.op("vector", lambda srep=srep, mxp=mxp: nc.vector.match_replace(
                    out=srep, in_to_replace=mxp, in_values=sgat[:, (tc % 2) * 8:(tc % 2) * 8 + 8], imm_value=-3e38),
                 reads=[f"g_mxp{tc % 2}", f"g_s{tc % 2}"], writes=[f"g_srep{tc % 2}", f"g_ev{tc % 2}"], tag=f"gmr1{tc}")
            P.op("vector", lambda srep=srep, tc=tc: nc.vector.tensor_single_scalar(
                    eq1a[:, tc * 8:(tc + 1) * 8], srep, -1e38, ALU.is_le),
                 reads=[f"g_srep{tc % 2}"], writes=[f"eq1_{tc}"], tag=f"geq1{tc}")
            P.op("vector", lambda mxp=mxp: nc.vector.memset(mxp, 1e38),
                 reads=[f"g_srep{tc % 2}"], writes=[f"g_mxp{tc % 2}"])
            P.op("vector", lambda mxp=mxp, mx8=mx8: nc.vector.tensor_copy(mxp[:, 0:1], mx8[:, 1:2]),
                 reads=[f"g_mx8{tc % 2}"], writes=[f"g_mxp{tc % 2}"])
            srep2 = gwork[:, gp_ + 48:gp_ + 56]
            P.op("vector", lambda srep=srep, srep2=srep2, mxp=mxp: nc.vector.match_replace(
                    out=srep2, in_to_replace=mxp, in_values=srep, imm_value=-3e38),
                 reads=[f"g_mxp{tc % 2}", f"g_srep{tc % 2}"], writes=[f"g_srep2{tc % 2}", f"g_ev{tc % 2}"], tag=f"gmr2{tc}")
            P.op("vector", lambda srep2=srep2, tc=tc: nc.vector.tensor_single_scalar(
                    eq2a[:, tc * 8:(tc + 1) * 8], srep2, -1e38, ALU.is_le),
                 reads=[f"g_srep2{tc % 2}"], writes=[f"eq2_{tc}"], tag=f"geq2{tc}")
            P.op("vector", lambda tc=tc: nc.vector.tensor_sub(
                    eq2a[:, tc * 8:(tc + 1) * 8], eq2a[:, tc * 8:(tc + 1) * 8],
                    eq1a[:, tc * 8:(tc + 1) * 8]),
                 reads=[f"eq1_{tc}", f"eq2_{tc}"], writes=[f"eq2_{tc}"])
            P.op("vector", lambda mx8=mx8, tc=tc: nc.vector.tensor_sub(
                    dda[:, tc:tc + 1], mx8[:, 0:1], mx8[:, 1:2]),
                 reads=[f"g_mx8{tc % 2}"], writes=[f"dd_{tc}"], tag=f"gdd{tc}")
        P.op("scalar", lambda: nc.scalar.activation(w1a[:, :], dda[:, :], AF.Sigmoid),
             reads=[f"dd_{tc}" for tc in range(NTC)], writes=["w1a"], tag="gsig1")
        P.op("scalar", lambda: nc.scalar.activation(w2a[:, :], dda[:, :], AF.Sigmoid,
                                                    scale=kc_m1),
             reads=[f"dd_{tc}" for tc in range(NTC)] + ["konst"], writes=["w2a"],
             tag="gsig2")
        for tc in range(NTC):
            gp_ = (tc % 2) * 128
            t1 = gwork[:, gp_ + 8:gp_ + 16]; t2 = gwork[:, gp_ + 16:gp_ + 24]
            P.op("vector", lambda tc=tc, t1=t1: nc.vector.tensor_scalar(
                    t1, eq1a[:, tc * 8:(tc + 1) * 8], w1a[:, tc:tc + 1], None, ALU.mult),
                 reads=[f"eq1_{tc}"], sreads=["w1a"], writes=[f"g_t1{tc % 2}"], tag=f"gwe1{tc}")
            P.op("vector", lambda tc=tc, t2=t2: nc.vector.tensor_scalar(
                    t2, eq2a[:, tc * 8:(tc + 1) * 8], w2a[:, tc:tc + 1], None, ALU.mult),
                 reads=[f"eq2_{tc}"], sreads=["w2a"], writes=[f"g_t2{tc % 2}"], tag=f"gwe2{tc}")
            web16 = web_s[:, (tc % 2) * 8:(tc % 2) * 8 + 8]
            P.op("vector", lambda t1=t1, t2=t2, web16=web16: nc.vector.tensor_tensor(
                    web16, t1, t2, ALU.add),
                 reads=[f"g_t1{tc % 2}", f"g_t2{tc % 2}"], writes=[f"g_web{tc % 2}"], tag=f"gwe{tc}")
            tb = 6 + (tc % 2)
            P.op("tensor", lambda web16=web16, tb=tb: nc.tensor.transpose(
                    bbank(tb, r1=8, w=128)[0:8, 0:128], web16, identb[:, :]),
                 reads=[f"g_web{tc % 2}", "identb"], writes=[f"pb{tb}"], tag=f"gwtr{tc}")
            P.op("vector", lambda tc=tc, tb=tb: nc.vector.tensor_copy(
                    weT[:, tc * 128:(tc + 1) * 128], bbank(tb, r1=8, w=128)[0:8, 0:128]),
                 reads=[f"pb{tb}"], writes=[f"weT{tc // 4}"], tag=f"gwev{tc}")

        for hc2 in range(2):
            for nt in range(NNT):
                hb = (hc2 * NNT + nt) % 2
                wb = 2 + hb
                P.op("tensor", lambda hc2=hc2, nt=nt, wb=wb: nc.tensor.matmul(
                        bank(wb), expand[:, hc2 * 128:(hc2 + 1) * 128],
                        weT[:, nt * 512:(nt + 1) * 512], start=True, stop=True),
                     reads=[f"weT{nt}", "expand"], writes=[f"pb{wb}"],
                     tag=f"wex{hc2}_{nt}")
                for kc in range(8):
                    P.op("tensor", lambda hc2=hc2, nt=nt, kc=kc, hb=hb: nc.tensor.matmul(
                            bank(hb), g3[:, kc * 256 + hc2 * 128:kc * 256 + (hc2 + 1) * 128],
                            kreg[:, kc * 2048 + nt * 512:kc * 2048 + (nt + 1) * 512],
                            start=(kc == 0), stop=(kc == 7)),
                         reads=[f"w1t{kc}", f"vf{kc}"], writes=[f"pb{hb}"],
                         tag=f"hmm{hc2}_{nt}_{kc}")
                sil = silst[:, hb * 512:(hb + 1) * 512]
                P.op("scalar", lambda hb=hb, hc2=hc2, sil=sil: nc.scalar.activation(
                        sil, bank(hb), AF.Silu, bias=b1_s[:, hc2:hc2 + 1]),
                     reads=[f"pb{hb}", "b1"], writes=[f"sil{hb}"],
                     tag=f"silu{hc2}_{nt}")
                P.op("vector", lambda hc2=hc2, nt=nt, wb=wb, sil=sil: nc.vector.tensor_tensor(
                        g1[:, hc2 * 2048 + nt * 512:hc2 * 2048 + (nt + 1) * 512],
                        bank(wb), sil, ALU.mult),
                     reads=[f"pb{wb}", f"sil{hb}"], writes=[f"hs{hc2}n{nt}"],
                     tag=f"hscale{hc2}_{nt}")
        for mc in range(8):
            for nt in range(NNT):
                db = (mc * NNT + nt) % 2
                P.op("tensor", lambda mc=mc, nt=nt, db=db: nc.tensor.matmul(
                        bank(db), g3[0:8, 4672 + mc * 128:4672 + (mc + 1) * 128],
                        weT[:, nt * 512:(nt + 1) * 512], start=True, stop=False),
                     reads=["b2m", f"weT{nt}"], writes=[f"pb{db}"],
                     tag=f"db2{mc}_{nt}")
                for hc2 in range(2):
                    P.op("tensor", lambda mc=mc, nt=nt, hc2=hc2, db=db: nc.tensor.matmul(
                            bank(db), g3[:, 2048 + hc2 * 1024 + mc * 128:2048 + hc2 * 1024 + (mc + 1) * 128],
                            g1[:, hc2 * 2048 + nt * 512:hc2 * 2048 + (nt + 1) * 512],
                            start=False, stop=(hc2 == 1)),
                         reads=[f"w2t{hc2}", f"hs{hc2}n{nt}"], writes=[f"pb{db}"],
                         tag=f"dmm{mc}_{nt}_{hc2}")
                P.op("vector", lambda mc=mc, nt=nt, db=db: nc.vector.tensor_tensor(
                        xnf[:, mc * 2048 + nt * 512:mc * 2048 + (nt + 1) * 512],
                        bank(db), kreg[:, mc * 2048 + nt * 512:mc * 2048 + (nt + 1) * 512],
                        ALU.add),
                     reads=[f"pb{db}", f"vf{mc}"], writes=[f"ffn{mc}n{nt}"],
                     tag=f"dev{mc}_{nt}")
        P.barrier()

        for tc in range(NTC):
            xre = kf[:, (tc % 2) * 1024:(tc % 2) * 1024 + 1024]
            P.op("sync", lambda s=s, tc=tc, xre=xre: nc.sync.dma_start(
                    xre, x_d[s, tc * 128:(tc + 1) * 128, :]),
                 writes=[f"xre{tc % 2}"], dma=True, tag=f"ldxre{s}_{tc}", sem=f"xre{tc % 2}")
            tb = 6 + (tc % 2)
            for mc in range(8):
                P.op("tensor", lambda mc=mc, tc=tc, tb=tb: nc.tensor.transpose(
                        bbank(tb, c0=mc * 128, w=128),
                        xnf[:, mc * 2048 + tc * 128:mc * 2048 + (tc + 1) * 128],
                        identb[:, :]),
                     reads=[f"ffn{mc}n{tc // 4}", "identb"], writes=[f"pb{tb}"],
                     tag=f"ftr{tc}_{mc}")
            for nt2 in range(2):
                ab = (tc * 2 + nt2) % 4
                for kc in range(8):
                    P.op("tensor", lambda tc=tc, nt2=nt2, kc=kc, ab=ab: nc.tensor.matmul(
                            bank(ab), qctx[:, kc * 2048 + tc * 128:kc * 2048 + (tc + 1) * 128],
                            wslot(1)[:, kc * 1024 + nt2 * 512:kc * 1024 + (nt2 + 1) * 512],
                            start=(kc == 0), stop=(kc == 7)),
                         reads=[f"w1c{kc}", f"q{kc}n{tc // 4}"], writes=[f"pb{ab}"],
                         tag=f"womm{tc}_{nt2}_{kc}")
                ob = kf[:, 2048 + (tc % 2) * 1024 + nt2 * 512:2048 + (tc % 2) * 1024 + (nt2 + 1) * 512]
                P.op("vector", lambda ob=ob, ab=ab, xre=xre, nt2=nt2: nc.vector.tensor_tensor(
                        ob, bank(ab), xre[:, nt2 * 512:(nt2 + 1) * 512], ALU.add),
                     reads=[f"pb{ab}", f"xre{tc % 2}"], writes=[f"ob{tc % 2}_{nt2}"],
                     tag=f"asm1_{tc}_{nt2}")
                P.op("vector", lambda ob=ob, tb=tb, nt2=nt2: nc.vector.tensor_tensor(
                        ob, bbank(tb, c0=nt2 * 512, w=512), ob, ALU.add),
                     reads=[f"pb{tb}", f"ob{tc % 2}_{nt2}"], writes=[f"ob{tc % 2}_{nt2}"],
                     tag=f"asm2_{tc}_{nt2}")
            P.op("sync", lambda s=s, tc=tc: nc.sync.dma_start(
                    out_d[s, tc * 128:(tc + 1) * 128, :],
                    kf[:, 2048 + (tc % 2) * 1024:2048 + (tc % 2) * 1024 + 1024]),
                 reads=[f"ob{tc % 2}_0", f"ob{tc % 2}_1"], dma=True,
                 tag=f"stout{s}_{tc}", sem=f"outst{tc % 2}")
        if s == 0:
            wload(0, WQ, "ldwq_s1")
        P.barrier()

    P.emit()
    return nc


def kernel(**inputs):
    x = np.ascontiguousarray(inputs["L0_attn_hidden_states"])
    wq = np.ascontiguousarray(inputs["L0_attn_Wq"])
    wk = np.ascontiguousarray(inputs["L0_attn_Wk"])
    wv = np.ascontiguousarray(inputs["L0_attn_Wv"])
    wo = np.ascontiguousarray(inputs["L0_attn_Wo"])
    relb = np.ascontiguousarray(inputs["L0_attn_rel_bias"])
    wvf = np.ascontiguousarray(inputs["L0_ffn_Wv"].T)
    bv = np.ascontiguousarray(inputs["L0_ffn_bv"])
    wg = inputs["L0_ffn_Wg"]
    bg = inputs["L0_ffn_bg"]
    wp = inputs["L0_ffn_Wp"]
    alpha = np.asarray(inputs["L0_ffn_alpha"]).reshape(())
    g_r = inputs["L0_ffn_g_r"]; b_r = inputs["L0_ffn_b_r"]
    g_c = inputs["L0_ffn_g_c"]; b_c = inputs["L0_ffn_b_c"]
    w1 = inputs["L0_ffn_W1"]
    b1 = inputs["L0_ffn_b1"]
    w2 = inputs["L0_ffn_W2"]
    b2 = np.ascontiguousarray(inputs["L0_ffn_b2"])

    wgp = np.concatenate([np.asarray(wg).T,
                          np.asarray(wp).transpose(2, 1, 0).reshape(1024, 64)],
                         axis=1).astype(np.float32)
    w1t = np.asarray(w1).transpose(2, 0, 1).reshape(1024, 256).astype(np.float32)
    w2t = np.asarray(w2).transpose(0, 2, 1).reshape(256, 1024).astype(np.float32)
    bvr = np.asarray(bv).reshape(8, 128).T.astype(np.float32)
    b1r = np.asarray(b1).reshape(256).reshape(2, 128).T.astype(np.float32)
    gp = np.zeros((1, 48), np.float32)
    gp[0, 0:8] = g_r; gp[0, 8:16] = b_r; gp[0, 16:24] = g_c; gp[0, 24:32] = b_c
    gp[0, 32:40] = bg; gp[0, 40] = alpha

    buckets = _bucket_table()
    t_idx = np.arange(4095)
    bt = buckets[(2047 - t_idx) + 2047]
    import ml_dtypes
    ohrev = np.zeros((32, 4095), dtype=ml_dtypes.bfloat16)
    ohrev[bt, t_idx] = 1
    identb = np.eye(128, dtype=ml_dtypes.bfloat16)
    identf = np.eye(128, dtype=np.float32)
    onesr = np.ones((1, 128), np.float32)
    expand = np.zeros((8, 256), dtype=ml_dtypes.bfloat16)
    for e in range(8):
        expand[e, e * 32:(e + 1) * 32] = 1
    parexp = np.zeros((128, 128), dtype=ml_dtypes.bfloat16)
    for c in range(8):
        for p in range(128):
            parexp[p, c * 16 + 2 * c + (1 if p >= 64 else 0)] = 1

    nc = build_program()

    common = dict(wq=wq, wk=wk, wv=wv, wo=wo, wvf=np.ascontiguousarray(wvf),
                  wgp=wgp, w1t=w1t, w2t=w2t, b2m=b2, bvr=bvr, b1r=b1r, gp=gp,
                  relb=relb, ohrev=ohrev, identb=identb, identf=identf,
                  onesr=onesr, expand=np.asarray(expand), parexp=np.asarray(parexp))
    in_maps = []
    for core in range(8):
        m = dict(common)
        m["x"] = np.ascontiguousarray(x[core * 2:(core + 1) * 2])
        in_maps.append(m)
    res = run_bass_kernel_spmd(nc, in_maps, core_ids=list(range(8)))
    kernel.last_results = res
    return np.concatenate([r["out"] for r in res.results], axis=0)


# version 06
# speedup vs baseline: 1.0373x; 1.0373x over previous
import dataclasses
import numpy as np
import concourse.bass as bass
import concourse.mybir as mybir
from concourse.bass_utils import run_bass_kernel_spmd

F32 = mybir.dt.float32
BF16 = mybir.dt.bfloat16
AF = mybir.ActivationFunctionType
ALU = mybir.AluOpType
AX = mybir.AxisListType

ENGINES = ("sync", "tensor", "vector", "scalar", "gpsimd")
TRUST_SAME_ENGINE = {"tensor"}

S_LEN = 2048
D = 1024
NH = 16
DK = 64
NCHUNK = 8
NTC = 16
NNT = 4
VST = 66
T_VSBT = 3968
A_VSBT = 127


class Prog:
    def __init__(self, nc):
        self.nc = nc
        self.ops = []
        self.buf = {}
        self.bar = {e: -1 for e in ENGINES}

    def _st(self, b):
        if b not in self.buf:
            self.buf[b] = {"writer": None, "readers": []}
        return self.buf[b]

    def op(self, engine, emit, reads=(), writes=(), sreads=(), dma=False, tag="",
           nobarrier=False, sem=None):
        idx = len(self.ops)
        deps = set()
        sdeps = set()
        for b in reads:
            st = self._st(b)
            if st["writer"] is not None:
                deps.add(st["writer"])
        for b in sreads:
            st = self._st(b)
            if st["writer"] is not None:
                deps.add(st["writer"])
                sdeps.add(st["writer"])
        for b in writes:
            st = self._st(b)
            if st["writer"] is not None:
                deps.add(st["writer"])
            deps.update(st["readers"])
        for e, last in self.bar.items():
            if last >= 0:
                deps.add(last)
        self.ops.append(dict(idx=idx, engine=engine, emit=emit, deps=deps,
                             sdeps=sdeps, dma=dma, tag=tag, sem=(sem or engine)))
        for b in reads:
            self._st(b)["readers"].append(idx)
        for b in sreads:
            self._st(b)["readers"].append(idx)
        for b in writes:
            self.buf[b] = {"writer": idx, "readers": []}
        if not nobarrier:
            self._last_nb = getattr(self, "_last_nb", {e: -1 for e in ENGINES})
            self._last_nb[engine] = idx
        return idx

    def barrier(self):
        last = getattr(self, "_last_nb", {e: -1 for e in ENGINES})
        self.bar = dict(last)

    def emit(self):
        nc = self.nc
        n = len(self.ops)

        def will_wait(d, o):
            de = self.ops[d]["engine"]
            if de != o["engine"]:
                return True
            if de in TRUST_SAME_ENGINE and d not in o["sdeps"]:
                return False
            return True

        need_inc = [False] * n
        for o in self.ops:
            if o["dma"]:
                need_inc[o["idx"]] = True
            for d in o["deps"]:
                if will_wait(d, o):
                    need_inc[d] = True
        semnames = []
        for o in self.ops:
            if o["sem"] not in semnames:
                semnames.append(o["sem"])
        count_after = {}
        totals = {sn: 0 for sn in semnames}
        for o in self.ops:
            if need_inc[o["idx"]]:
                totals[o["sem"]] += 16 if o["dma"] else 1
            count_after[o["idx"]] = totals[o["sem"]]
        waits = []
        for o in self.ops:
            w = {}
            for d in o["deps"]:
                if not will_wait(d, o):
                    continue
                dsn = self.ops[d]["sem"]
                c = count_after[d]
                if c > 0:
                    w[dsn] = max(w.get(dsn, 0), c)
            waits.append(w)
        low = {e: {sn: 0 for sn in semnames} for e in ENGINES}
        pruned = []
        for o in self.ops:
            e = o["engine"]
            w2 = []
            for se, v in sorted(waits[o["idx"]].items()):
                if low[e][se] < v:
                    w2.append((se, v))
                    low[e][se] = v
            pruned.append(w2)

        sems = {sn: nc.alloc_semaphore(f"sem_{sn}") for sn in semnames}
        per_engine = {e: [] for e in ENGINES}
        for o in self.ops:
            per_engine[o["engine"]].append(o)

        def runner(eng_name):
            def run(eng):
                for o in per_engine[eng_name]:
                    for (se, v) in pruned[o["idx"]]:
                        eng.wait_ge(sems[se], v)
                    ins = o["emit"]()
                    if need_inc[o["idx"]]:
                        amt = 16 if o["dma"] else 1
                        ins.then_inc(sems[o["sem"]], amt)
                if eng_name == "sync":
                    for sn in semnames:
                        if totals[sn] > 0:
                            eng.wait_ge(sems[sn], totals[sn])
            return run

        with nc.Block() as block:
            block.sync(runner("sync"))
            block.tensor(runner("tensor"))
            block.vector(runner("vector"))
            block.scalar(runner("scalar"))
            block.gpsimd(runner("gpsimd"))


def _bucket_table():
    nb = 16
    max_exact = 8
    delta = np.arange(-2047, 2048, dtype=np.int64)
    buckets = (delta > 0).astype(np.int64) * nb
    rp = np.abs(delta)
    is_small = rp < max_exact
    log_ratio = (np.log(np.maximum(rp, 1).astype(np.float32) / max_exact)
                 / np.float32(np.log(128.0 / max_exact)))
    large = max_exact + (log_ratio * (nb - max_exact)).astype(np.int32)
    large = np.minimum(large, nb - 1)
    return (buckets + np.where(is_small, rp, large)).astype(np.int64)


def build_program():
    nc = bass.Bass("TRN2", target_bir_lowering=False)
    din = lambda name, shape, dt=F32: nc.dram_tensor(name, shape, dt, kind="ExternalInput").ap()

    x_d = din("x", [2, S_LEN, D])
    wq_d = din("wq", [D, D]);  wk_d = din("wk", [D, D])
    wv_d = din("wv", [D, D]);  wo_d = din("wo", [D, D])
    wvf_d = din("wvf", [D, D])
    wgp_d = din("wgp", [D, 72])
    w1t_d = din("w1t", [D, 256]); w2t_d = din("w2t", [256, D])
    b2m_d = din("b2m", [8, D])
    bvr_d = din("bvr", [128, 8]); b1r_d = din("b1r", [128, 2])
    gp_d = din("gp", [1, 48])
    relb_d = din("relb", [32, 16])
    ohrev_d = din("ohrev", [32, 4095], BF16)
    identb_d = din("identb", [128, 128], BF16)
    identf_d = din("identf", [128, 128])
    onesr_d = din("onesr", [1, 128])
    expand_d = din("expand", [8, 256], BF16)
    parexp_d = din("parexp", [128, 128], BF16)

    out_d = nc.dram_tensor("out", [2, S_LEN, D], F32, kind="ExternalOutput").ap()
    vsbt_d = nc.dram_tensor("vsbt_dram", [16, 128, T_VSBT], BF16, kind="Internal").ap()
    vf_d = nc.dram_tensor("vf_dram", [8, 128, 2048], BF16, kind="Internal").ap()

    P = Prog(nc)
    SB = lambda name, shape, dt: nc.alloc_sbuf_tensor(name, shape, dt)

    xnf = SB("xnf", [128, 16384], BF16)
    qctx = SB("qctx", [128, 16384], BF16)
    kreg = SB("kreg", [128, 16384], BF16)
    vtok = SB("vtok", [128, 16 * 1056], BF16)
    g1 = SB("g1", [128, 24576], BF16)
    g3 = SB("g3", [128, 8192], BF16)

    negsh = SB("negsh", [128, 16], F32)
    konst = SB("konst", [128, 8], F32)
    bv_s = SB("bv_s", [128, 8], F32)
    b1_s = SB("b1_s", [128, 2], F32)
    grb = SB("grb", [128, 8], F32); brb = SB("brb", [128, 8], F32)
    gcb = SB("gcb", [128, 8], F32); bcb = SB("bcb", [128, 8], F32)
    bgb = SB("bgb", [128, 8], F32)
    sigb = SB("sigb", [128, 1], F32)
    identb = SB("identb_s", [128, 128], BF16)
    identf = SB("identf_s", [128, 128], F32)
    onesr = SB("onesr_s", [1, 128], F32)
    expand = SB("expand_s", [8, 256], BF16)
    parexp = SB("parexp_s", [128, 128], BF16)
    relb = SB("relb_s", [32, 16], BF16)
    gp_s = SB("gp_s", [1, 48], F32)
    vstage = SB("vstage", [128, 1024], BF16)
    ctxst = SB("ctxst", [128, 520], BF16)
    silst = SB("silst", [128, 1024], BF16)
    sgat = SB("sgat", [128, 64], F32)
    gwork = SB("gwork", [128, 1024], F32)
    eq1a = SB("eq1a", [128, 128], F32)
    eq2a = SB("eq2a", [128, 128], F32)
    dda = SB("dda", [128, 16], F32)
    w1a = SB("w1a", [128, 16], F32)
    w2a = SB("w2a", [128, 16], F32)
    lnw = SB("lnw", [128, 40], F32)
    zc = SB("zc", [128, 8], F32)
    m16 = SB("m16", [16, 16], F32)
    r16 = SB("r16", [1, 32], F32)
    web_s = SB("web_s", [128, 64], BF16)

    psall = nc.alloc_psum_tensor("psall", [128, 4096], F32)
    psap = psall.ap()
    psb = psap.bitcast(BF16)

    def bank(i, r0=0, r1=128, c0=0, w=512):
        return psall[r0:r1, i * 512 + c0:i * 512 + c0 + w]

    def wide(i, c0=0, w=1024):
        return psall[:, i * 1024 + c0:i * 1024 + c0 + w]

    def bbank(i, r0=0, r1=128, c0=0, w=1024):
        return psb[r0:r1, i * 1024 + c0:i * 1024 + c0 + w]

    g1f = g1.ap().bitcast(F32)
    g3f = g3.ap().bitcast(F32)
    kf = kreg.ap().bitcast(F32)

    KV = {}
    def kcol(v):
        if v not in KV:
            i = len(KV)
            KV[v] = i
            P.op("vector", lambda v=v, i=i: nc.vector.memset(konst[:, i:i + 1], v),
                 writes=["konst"], tag=f"konst{i}")
        return konst[:, KV[v]:KV[v] + 1]

    def ld(engine, dst, src, key, tag):
        P.op(engine, lambda: getattr(nc, engine).dma_start(dst, src),
             writes=[key], dma=True, tag=tag, sem="s_" + key)

    ld("sync", identf[:, :], identf_d[:, :], "identf", "ld_identf")
    ld("sync", onesr[:, :], onesr_d[:, :], "onesr", "ld_onesr")
    ld("sync", gp_s[:, :], gp_d[:, :], "gp", "ld_gp")
    ld("sync", bv_s[:, :], bvr_d[:, :], "bv", "ld_bv")
    ld("sync", b1_s[:, :], b1r_d[:, :], "b1", "ld_b1")
    ld("sync", identb[:, :], identb_d[:, :], "identb", "ld_identb")
    ld("sync", expand[:, :], expand_d[:, :], "expand", "ld_expand")
    ld("sync", parexp[:, :], parexp_d[:, :], "parexp", "ld_parexp")
    ld("gpsimd", relb[:, :], relb_d[:, :], "relb", "ld_relb")
    ohrev_v = g1[0:32, 16384:16384 + 4095]
    ld("sync", ohrev_v, ohrev_d[:, :], "ohrev", "ld_ohrev")
    vrev_v = g1[0:16, 20480:20480 + 4095]

    kc_eps = kcol(1e-5); kc_8 = kcol(0.125); kc_nh = kcol(-0.5)
    kc_ph = kcol(0.5); kc_m1 = kcol(-1.0)

    P.op("tensor", lambda: nc.tensor.matmul(bank(6, c0=0, w=40)[:, 0:40], onesr[0:1, :],
                                            gp_s[0:1, 0:40], start=True, stop=True),
         reads=["onesr", "gp"], writes=["pb6"], tag="gpbcast")
    for t, key, c0 in [(grb, "grb", 0), (brb, "brb", 8), (gcb, "gcb", 16),
                       (bcb, "bcb", 24), (bgb, "bgb", 32)]:
        P.op("vector", lambda t=t, c0=c0: nc.vector.tensor_copy(t[:, :], bank(6)[:, c0:c0 + 8]),
             reads=["pb6"], writes=[key], tag=f"bc_{key}")
    siga = SB("siga", [1, 1], F32)
    P.op("scalar", lambda: nc.scalar.activation(siga[:, :], gp_s[0:1, 40:41], AF.Sigmoid),
         reads=["gp"], writes=["siga"], tag="sigalpha")
    P.op("tensor", lambda: nc.tensor.matmul(bank(7, c0=0, w=1)[:, 0:1], onesr[0:1, :],
                                            siga[0:1, :], start=True, stop=True),
         reads=["onesr", "siga"], writes=["pb7"], tag="sigbcast")
    P.op("vector", lambda: nc.vector.tensor_copy(sigb[:, :], bank(7)[:, 0:1]),
         reads=["pb7"], writes=["sigb"], tag="sigevac")

    for nt in range(8):
        w = 512 if nt < 7 else 511
        b = 4 + (nt % 2)
        P.op("tensor", lambda nt=nt, w=w, b=b: nc.tensor.matmul(
                bank(b, r1=16, w=w), relb[:, :], ohrev_v[:, nt * 512:nt * 512 + w],
                start=True, stop=True),
             reads=["relb", "ohrev"], writes=[f"pb{b}"], tag=f"vrev{nt}")
        P.op("scalar", lambda nt=nt, w=w, b=b: nc.scalar.activation(
                vrev_v[:, nt * 512:nt * 512 + w], bank(b, r1=16, w=w), AF.Exp),
             reads=[f"pb{b}"], writes=["vrev"], tag=f"vrevexp{nt}")
    for j in range(4):
        src = dataclasses.replace(
            g1.ap(), offset=20480 + A_VSBT - 32 * j,
            ap=[[24576, 16], [-1, 32], [1, T_VSBT]])
        P.op("gpsimd", lambda j=j, src=src: nc.gpsimd.dma_start(
                vsbt_d[:, 32 * j:32 * (j + 1), :], src),
             reads=["vrev"], writes=[f"vsbtd{j}"], dma=True, tag=f"vsbtbuild{j}",
             sem="vsbtb")

    WQ, WK, WV, WVF, WO = wq_d, wk_d, wv_d, wvf_d, wo_d

    def wslot(s):
        return g1[:, s * 8192:(s + 1) * 8192]

    def wload(slot, wdram, tag):
        dst = wslot(slot)
        for c in range(8):
            P.op("gpsimd", lambda c=c, dst=dst, wdram=wdram: nc.gpsimd.dma_start(
                    dst[:, c * 1024:(c + 1) * 1024],
                    wdram[c * 128:(c + 1) * 128, :]),
                 reads=([f"w{slot}c{c - 2}"] if c >= 2 else []),
                 writes=[f"w{slot}c{c}"], dma=True, tag=f"{tag}c{c}", nobarrier=True,
                 sem=f"wl{slot}{c % 2}")

    def vsbt_load(s, h, tag):
        slot = h % 2
        dst = g1[:, 16384 + slot * T_VSBT:16384 + (slot + 1) * T_VSBT]
        P.op("sync", lambda dst=dst, h=h: nc.sync.dma_start(dst, vsbt_d[h, :, :]),
             reads=["vsbtd0", "vsbtd1", "vsbtd2", "vsbtd3"],
             writes=[f"vsbt{slot}"], dma=True, tag=tag, nobarrier=True, sem=f"vsbt{slot}")

    wload(0, WQ, "ldwq_s0")

    for s in range(2):
        def ln_stats(tc):
            xin = g3f[:, (tc % 2) * 1024:(tc % 2) * 1024 + 1024]
            xkey = f"xin{tc % 2}"
            P.op("sync", lambda s=s, tc=tc, xin=xin: nc.sync.dma_start(
                    xin, x_d[s, tc * 128:(tc + 1) * 128, :]),
                 writes=[xkey], dma=True, tag=f"ldx{s}_{tc}", sem=f"xin{tc % 2}")
            lb = (tc % 2) * 5
            sums = lnw[:, lb:lb + 1]; ssq = lnw[:, lb + 1:lb + 2]; mean = lnw[:, lb + 2:lb + 3]
            var = lnw[:, lb + 3:lb + 4]; rstd = lnw[:, lb + 4:lb + 5]
            scr = g3f[:, 3072:4096]
            P.op("vector", lambda xin=xin, sums=sums: nc.vector.reduce_sum(sums, xin, AX.X),
                 reads=[xkey], writes=[f"ln_sums{tc % 2}"], tag=f"lnsum{tc}")
            P.op("vector", lambda xin=xin, scr=scr: nc.vector.scalar_tensor_tensor(
                    scr, xin, 0.0, xin, ALU.add, ALU.mult),
                 reads=[xkey], writes=["lnscr"], tag=f"lnsq{tc}")
            P.op("vector", lambda scr=scr, ssq=ssq: nc.vector.reduce_sum(ssq, scr, AX.X),
                 reads=["lnscr"], writes=[f"ln_ssq{tc % 2}"], tag=f"lnssq{tc}")
            P.op("vector", lambda sums=sums, mean=mean: nc.vector.tensor_scalar_mul(
                    mean, sums, 1.0 / 1024), reads=[f"ln_sums{tc % 2}"], writes=[f"ln_mean{tc % 2}"])
            P.op("vector", lambda mean=mean, var=var: nc.vector.scalar_tensor_tensor(
                    var, mean, -1.0, mean, ALU.mult, ALU.mult),
                 reads=[f"ln_mean{tc % 2}"], writes=[f"ln_var{tc % 2}"])
            P.op("vector", lambda ssq=ssq, var=var: nc.vector.scalar_tensor_tensor(
                    var, ssq, 1.0 / 1024, var, ALU.mult, ALU.add),
                 reads=[f"ln_ssq{tc % 2}", f"ln_var{tc % 2}"], writes=[f"ln_var{tc % 2}"])
            P.op("scalar", lambda var=var, rstd=rstd: nc.scalar.activation(
                    rstd, var, AF.Ln, bias=kc_eps), reads=[f"ln_var{tc % 2}", "konst"],
                 writes=[f"ln_rstd{tc % 2}"], tag=f"lnln{tc}")
            P.op("scalar", lambda rstd=rstd: nc.scalar.activation(
                    rstd, rstd, AF.Exp, scale=kc_nh),
                 sreads=[f"ln_rstd{tc % 2}"], reads=["konst"], writes=[f"ln_rstd{tc % 2}"],
                 tag=f"lnexp{tc}")

        def ln_apply(tc):
            xin = g3f[:, (tc % 2) * 1024:(tc % 2) * 1024 + 1024]
            xkey = f"xin{tc % 2}"
            lb = (tc % 2) * 5
            mean = lnw[:, lb + 2:lb + 3]; rstd = lnw[:, lb + 4:lb + 5]
            xnt = g3[:, 4096 + (tc % 2) * 1024:4096 + (tc % 2) * 1024 + 1024]
            xntk = f"xnt{tc % 2}"
            P.op("vector", lambda xin=xin, xnt=xnt, mean=mean, rstd=rstd:
                    nc.vector.tensor_scalar(xnt, xin, mean, rstd, ALU.subtract, ALU.mult),
                 reads=[xkey], sreads=[f"ln_mean{tc % 2}", f"ln_rstd{tc % 2}"], writes=[xntk],
                 tag=f"lnapply{tc}")
            tb = 6 + (tc % 2)
            for c in range(8):
                P.op("tensor", lambda c=c, tb=tb, xnt=xnt: nc.tensor.transpose(
                        bbank(tb, c0=c * 128, w=128), xnt[:, c * 128:(c + 1) * 128],
                        identb[:, :]),
                     reads=[xntk, "identb"], writes=[f"pb{tb}"], tag=f"lntr{tc}_{c}")
            dst = dataclasses.replace(xnf.ap(), offset=tc * 128,
                                      ap=[[16384, 128], [2048, 8], [1, 128]])
            P.op("vector", lambda dst=dst, tb=tb: nc.vector.tensor_copy(
                    dst, bbank(tb).rearrange("p (c m) -> p c m", c=8)),
                 reads=[f"pb{tb}"], writes=[f"xnf{tc}"], tag=f"lnevac{tc}")

        for tc in range(NTC):
            ln_stats(tc)
            if tc >= 1:
                ln_apply(tc - 1)
        ln_apply(NTC - 1)
        P.barrier()

        wload(1, WK, f"ldwk_s{s}")
        xnf_reads = [f"xnf{tc}" for tc in range(NTC)]

        def proj_feat(slot, dest, destname, evac_engine, tag):
            for mc in range(8):
                for nt in range(NNT):
                    b = (mc * NNT + nt) % 4
                    for kc in range(8):
                        P.op("tensor", lambda mc=mc, nt=nt, kc=kc, b=b, slot=slot:
                                nc.tensor.matmul(
                                    bank(b), wslot(slot)[:, kc * 1024 + mc * 128:kc * 1024 + (mc + 1) * 128],
                                    xnf[:, kc * 2048 + nt * 512:kc * 2048 + (nt + 1) * 512],
                                    start=(kc == 0), stop=(kc == 7)),
                             reads=[f"w{slot}c{kc}"] + xnf_reads, writes=[f"pb{b}"],
                             tag=f"{tag}mm{mc}_{nt}_{kc}")
                    if evac_engine == "scalar":
                        P.op("scalar", lambda mc=mc, nt=nt, b=b, dest=dest:
                                nc.scalar.activation(
                                    dest[:, mc * 2048 + nt * 512:mc * 2048 + (nt + 1) * 512],
                                    bank(b), AF.Copy),
                             reads=[f"pb{b}"], writes=[f"{destname}{mc}n{nt}"],
                             tag=f"{tag}ev{mc}_{nt}")
                    else:
                        P.op("vector", lambda mc=mc, nt=nt, b=b, dest=dest:
                                nc.vector.tensor_copy(
                                    dest[:, mc * 2048 + nt * 512:mc * 2048 + (nt + 1) * 512],
                                    bank(b)),
                             reads=[f"pb{b}"], writes=[f"{destname}{mc}n{nt}"],
                             tag=f"{tag}ev{mc}_{nt}")

        proj_feat(0, qctx, "q", "scalar", f"qp{s}")
        proj_feat(1, kreg, "k", "scalar", f"kp{s}")
        wload(0, WV, f"ldwv_s{s}")

        for tc in range(NTC):
            P.op("vector", lambda tc=tc: nc.vector.memset(
                    dataclasses.replace(vtok.ap(), offset=tc * 1056 + 64,
                                        ap=[[16896, 128], [VST, 16], [1, 1]]), 1.0),
                 writes=[f"vt{tc}"], tag=f"vones{tc}")
            for half in range(2):
                b = (tc * 2 + half) % 4
                for kc in range(8):
                    P.op("tensor", lambda tc=tc, half=half, kc=kc, b=b:
                            nc.tensor.matmul(
                                bank(b), xnf[:, kc * 2048 + tc * 128:kc * 2048 + (tc + 1) * 128],
                                wslot(0)[:, kc * 1024 + half * 512:kc * 1024 + (half + 1) * 512],
                                start=(kc == 0), stop=(kc == 7)),
                         reads=[f"w0c{kc}"] + xnf_reads, writes=[f"pb{b}"],
                         tag=f"vp{s}mm{tc}_{half}_{kc}")
                dst = dataclasses.replace(
                    vtok.ap(), offset=tc * 1056 + half * 8 * VST,
                    ap=[[16896, 128], [VST, 8], [1, 64]])
                P.op("vector", lambda dst=dst, b=b: nc.vector.tensor_copy(
                        dst, bank(b).rearrange("p (g x) -> p g x", g=8)),
                     reads=[f"pb{b}"], writes=[f"vt{tc}"], tag=f"vev{tc}_{half}")
        wload(1, WVF, f"ldwvf_s{s}")

        for mc in range(8):
            for nt in range(NNT):
                b = (mc * NNT + nt) % 4
                for kc in range(8):
                    P.op("tensor", lambda mc=mc, nt=nt, kc=kc, b=b: nc.tensor.matmul(
                            bank(b), wslot(1)[:, kc * 1024 + mc * 128:kc * 1024 + (mc + 1) * 128],
                            xnf[:, kc * 2048 + nt * 512:kc * 2048 + (nt + 1) * 512],
                            start=(kc == 0), stop=(kc == 7)),
                         reads=[f"w1c{kc}"] + xnf_reads, writes=[f"pb{b}"],
                         tag=f"vfmm{mc}_{nt}_{kc}")
                vs = vstage[:, (nt % 2) * 512:(nt % 2) * 512 + 512]
                P.op("vector", lambda mc=mc, b=b, vs=vs: nc.vector.tensor_scalar(
                        vs, bank(b), bv_s[:, mc:mc + 1], None, ALU.add),
                     reads=[f"pb{b}", "bv"], writes=[f"vstage{nt % 2}"],
                     tag=f"vfev{mc}_{nt}")
                P.op("sync", lambda mc=mc, nt=nt, vs=vs: nc.sync.dma_start(
                        vf_d[mc, :, nt * 512:(nt + 1) * 512], vs),
                     reads=[f"vstage{nt % 2}"], writes=[f"vfd{mc}n{nt}"], dma=True,
                     tag=f"vfst{mc}_{nt}", sem=f"vfst{nt % 2}")

        P.barrier()
        for side, reg, regk in ((0, qctx, "q"), (1, kreg, "k")):
            for c in range(8):
                sq = g3[:, (c % 2) * 2048:(c % 2) * 2048 + 2048]
                P.op("vector", lambda c=c, reg=reg, sq=sq: nc.vector.scalar_tensor_tensor(
                        sq, reg[:, c * 2048:(c + 1) * 2048], 0.0,
                        reg[:, c * 2048:(c + 1) * 2048], ALU.add, ALU.mult),
                     reads=[f"{regk}{c}n{j}" for j in range(4)],
                     writes=[f"sq{c % 2}"], tag=f"sq{side}_{c}")
                for nt in range(NNT):
                    qb = 4 + 2 * side + nt // 2
                    qbase = 64 * (nt % 2)
                    P.op("tensor", lambda c=c, nt=nt, sq=sq, qb=qb, qbase=qbase:
                            nc.tensor.matmul(
                                bank(qb, r0=qbase, r1=qbase + 16),
                                parexp[:, c * 16:(c + 1) * 16], sq[:, nt * 512:(nt + 1) * 512],
                                start=(c == 0 and nt % 2 == 0), stop=(c == 7),
                                tile_position=(0, qbase)),
                         reads=[f"sq{c % 2}"], writes=[f"pb{qb}"],
                         tag=f"q2mm{side}_{c}_{nt}")
            for nt in range(NNT):
                qb = 4 + 2 * side + nt // 2
                qbase = 64 * (nt % 2)
                P.op("vector", lambda nt=nt, side=side, qb=qb, qbase=qbase:
                        nc.vector.reduce_max(
                            m16[:, side * 8 + nt:side * 8 + nt + 1],
                            bank(qb, r0=qbase, r1=qbase + 16), AX.X),
                     reads=[f"pb{qb}"], writes=[f"m16_{side}"],
                     tag=f"q2red{side}_{nt}")

        vsbt_load(s, 0, f"vsbt{s}_0")
        vsbt_load(s, 1, f"vsbt{s}_1")
        P.op("vector", lambda: nc.vector.reduce_max(m16[0:16, 4:5], m16[0:16, 0:4], AX.X),
             reads=["m16_0"], writes=["mq2c"], tag="mq2c")
        P.op("vector", lambda: nc.vector.reduce_max(m16[0:16, 5:6], m16[0:16, 8:12], AX.X),
             reads=["m16_1"], writes=["mk2c"], tag="mk2c")
        P.op("vector", lambda: nc.vector.tensor_mul(m16[0:16, 6:7], m16[0:16, 4:5],
                                                    m16[0:16, 5:6]),
             reads=["mq2c", "mk2c"], writes=["prod2"], tag="prod2")
        P.op("scalar", lambda: nc.scalar.activation(m16[0:16, 7:8], m16[0:16, 6:7],
                                                    AF.Ln), reads=["prod2"],
             writes=["lnprod"], tag="lnprod")
        P.op("scalar", lambda: nc.scalar.activation(m16[0:16, 7:8], m16[0:16, 7:8],
                                                    AF.Exp, scale=kc_ph[0:16]),
             sreads=["lnprod"], reads=["konst"], writes=["lnprod"], tag="shifth")
        P.op("scalar", lambda: nc.scalar.drain(), reads=["lnprod"], writes=["shdrA"],
             tag="shdrainA")
        P.op("tensor", lambda: nc.tensor.transpose(bank(4, r1=1, w=16)[0:1, 0:16],
                                                   m16[0:16, 7:8], identf[0:16, 0:16]),
             reads=["lnprod", "identf", "shdrA", "vsbt0"], writes=["pb4"], tag="shtr")
        P.op("vector", lambda: nc.vector.tensor_copy(r16[0:1, 0:16], bank(4, r1=1, w=16)[0:1, 0:16]),
             reads=["pb4"], writes=["r16"], tag="shrow")
        P.op("vector", lambda: nc.vector.drain(), reads=["r16"], writes=["shdrB"],
             tag="shdrainB")
        P.op("tensor", lambda: nc.tensor.matmul(bank(5, w=16)[:, 0:16], onesr[0:1, :],
                                                r16[0:1, 0:16], start=True, stop=True),
             reads=["onesr", "r16", "shdrB", "vsbt1"], writes=["pb5"], tag="shbcast")
        P.op("vector", lambda: nc.vector.tensor_scalar_mul(negsh[:, :], bank(5, w=16)[:, 0:16],
                                                           -1.0),
             reads=["pb5"], writes=["negsh"], tag="negsh")
        P.barrier()

        def sc_quad(hp, qt, quad):
            h0, h1, hc = 2 * hp, 2 * hp + 1, hp
            q0 = qt * 512
            tb = quad % 2
            for kc4 in range(4):
                kc = quad * 4 + kc4
                for hi, h in enumerate((h0, h1)):
                    base = (h % 2) * 64
                    sb = hi * 3 + kc % 3
                    P.op("tensor", lambda kc=kc, base=base, sb=sb, q0=q0, hc=hc:
                            nc.tensor.matmul(
                                bank(sb),
                                kreg[base:base + 64,
                                     hc * 2048 + kc * 128:hc * 2048 + (kc + 1) * 128],
                                qctx[base:base + 64, hc * 2048 + q0:hc * 2048 + q0 + 512],
                                start=True, stop=True, tile_position=(base, 0)),
                         reads=[f"k{hc}n{kc // 4}", f"q{hc}n{qt}"],
                         writes=[f"pb{sb}"], tag=f"sc{h}_{qt}_{kc}")
                for hi, h in enumerate((h0, h1)):
                    sb = hi * 3 + kc % 3
                    texp = g3[:, (hi * 2 + tb) * 2048:(hi * 2 + tb) * 2048 + 2048]
                    P.op("scalar", lambda sb=sb, h=h, kc4=kc4, texp=texp:
                            nc.scalar.activation(
                                texp[:, kc4 * 512:(kc4 + 1) * 512], bank(sb),
                                AF.Exp, bias=negsh[:, h:h + 1]),
                         reads=[f"pb{sb}", "negsh"],
                         writes=[f"texp{hi}b{tb}k{kc4}"], tag=f"exp{h}_{qt}_{kc}")
            for hi, h in enumerate((h0, h1)):
                texp = g3[:, (hi * 2 + tb) * 2048:(hi * 2 + tb) * 2048 + 2048]
                slot = h % 2
                off = 16384 + slot * T_VSBT + (1920 - (quad * 4) * 128 + q0)
                win = dataclasses.replace(g1.ap(), offset=off,
                                          ap=[[24576, 128], [-128, 4], [1, 512]])
                wt = g1[:, hi * 8192 + quad * 2048:hi * 8192 + (quad + 1) * 2048]
                P.op("vector", lambda texp=texp, win=win, wt=wt: nc.vector.tensor_tensor(
                        wt.rearrange("p (a b) -> p a b", a=4),
                        texp.rearrange("p (a b) -> p a b", a=4), win, ALU.mult),
                     reads=[f"texp{hi}b{tb}k{j}" for j in range(4)] + [f"vsbt{slot}"],
                     writes=[f"wt{hi}q{quad}"], tag=f"mult{h}_{qt}_{quad}")

        def ctx_group(hp, qt, quad):
            h0, h1 = 2 * hp, 2 * hp + 1
            for hi, h in enumerate((h0, h1)):
                cb = 6 + hi
                for qc in range(4):
                    for kc in range(quad * 4, quad * 4 + 4):
                        P.op("tensor", lambda hi=hi, h=h, qc=qc, kc=kc, cb=cb:
                                nc.tensor.matmul(
                                    bank(cb, c0=qc * 65, w=65),
                                    g1[:, hi * 8192 + kc * 512 + qc * 128:
                                       hi * 8192 + kc * 512 + (qc + 1) * 128],
                                    vtok[:, kc * 1056 + h * VST:kc * 1056 + h * VST + 65],
                                    start=(qc == 0 and kc == 0), stop=(kc == 15)),
                             reads=[f"wt{hi}q{kc // 4}", f"vt{kc}"],
                             writes=[f"pb{cb}"], tag=f"ctx{h}_{qt}_{qc}_{kc}")

        def ctx_evac(hp, qt):
            h0, h1, hc = 2 * hp, 2 * hp + 1, hp
            for hi, h in enumerate((h0, h1)):
                cb = 6 + hi
                zap = dataclasses.replace(psap, offset=cb * 512 + 64,
                                          ap=[[4096, 128], [65, 4], [1, 1]])
                P.op("vector", lambda zap=zap, hi=hi: nc.vector.reciprocal(
                        zc[:, hi * 4:hi * 4 + 4], zap),
                     reads=[f"pb{cb}"], writes=[f"zc{hi}"], tag=f"zrec{h}_{qt}")
                cst = ctxst[:, hi * 260:hi * 260 + 260]
                for qc in range(4):
                    P.op("vector", lambda qc=qc, cb=cb, cst=cst, hi=hi:
                            nc.vector.tensor_scalar(
                                cst[:, qc * 64:(qc + 1) * 64],
                                bank(cb, c0=qc * 65, w=64),
                                zc[:, hi * 4 + qc:hi * 4 + qc + 1], None, ALU.mult),
                         reads=[f"pb{cb}"], sreads=[f"zc{hi}"],
                         writes=[f"cst{hi}"], tag=f"ctxsc{h}_{qt}_{qc}")
                tb = cb
                for qc in range(4):
                    P.op("tensor", lambda qc=qc, cst=cst, tb=tb: nc.tensor.transpose(
                            bbank(tb, r1=64, c0=qc * 128, w=128),
                            cst[:, qc * 64:(qc + 1) * 64], identb[:, :]),
                         reads=[f"cst{hi}", "identb"], writes=[f"pb{tb}"],
                         tag=f"ctxtr{h}_{qt}_{qc}")
                base = (h % 2) * 64
                P.op("vector", lambda tb=tb, base=base, qt=qt, hc=hc: nc.vector.tensor_copy(
                        qctx[base:base + 64, hc * 2048 + qt * 512:hc * 2048 + (qt + 1) * 512],
                        bbank(tb, r1=64, w=512)[0:64, 0:512]),
                     reads=[f"pb{tb}"], writes=[f"q{hc}n{qt}"],
                     tag=f"ctxev{h}_{qt}")

        tiles = [(hp, qt) for hp in range(8) for qt in range(4)]
        for quad in range(4):
            sc_quad(0, 0, quad)
        for ti, (hp, qt) in enumerate(tiles):
            nxt = tiles[ti + 1] if ti + 1 < len(tiles) else None
            for quad in range(4):
                ctx_group(hp, qt, quad)
                if nxt is not None:
                    sc_quad(nxt[0], nxt[1], quad)
            ctx_evac(hp, qt)
            if qt == 2:
                if hp < 7:
                    vsbt_load(s, 2 * hp + 2, f"vsbt{s}_{2 * hp + 2}")
                    vsbt_load(s, 2 * hp + 3, f"vsbt{s}_{2 * hp + 3}")
                elif s == 0:
                    vsbt_load(1, 0, "vsbt1_0")
                    vsbt_load(1, 1, "vsbt1_1")
        P.barrier()

        for mc in range(8):
            P.op("sync", lambda mc=mc: nc.sync.dma_start(
                    kreg[:, mc * 2048:(mc + 1) * 2048], vf_d[mc, :, :]),
                 reads=[f"vfd{mc}n{j}" for j in range(4)]
                       + ([f"vf{mc - 2}"] if mc >= 2 else []),
                 writes=[f"vf{mc}"], dma=True, tag=f"ldvf{mc}", sem=f"vfld{mc % 2}")
        for c in range(8):
            P.op("gpsimd", lambda c=c: nc.gpsimd.dma_start(
                    g3[:, c * 256:(c + 1) * 256], w1t_d[c * 128:(c + 1) * 128, :]),
                 writes=[f"w1t{c}", "w1tCH"], dma=True, tag=f"ldw1t{c}", sem="w1t")
        for c in range(2):
            P.op("gpsimd", lambda c=c: nc.gpsimd.dma_start(
                    g3[:, 2048 + c * 1024:2048 + (c + 1) * 1024],
                    w2t_d[c * 128:(c + 1) * 128, :]),
                 writes=[f"w2t{c}", "w2tCH"], dma=True, tag=f"ldw2t{c}", sem="w2t")
        for c in range(8):
            P.op("gpsimd", lambda c=c: nc.gpsimd.dma_start(
                    g3[:, 4096 + c * 72:4096 + (c + 1) * 72],
                    wgp_d[c * 128:(c + 1) * 128, :]),
                 writes=[f"wgp{c}", "wgpCH"], dma=True, tag=f"ldwgp{c}", sem="wgp")
        P.op("gpsimd", lambda: nc.gpsimd.dma_start(g3[0:8, 4672:4672 + 1024], b2m_d[:, :]),
             writes=["b2m"], dma=True, tag="ldb2m", sem="s_b2m")
        wload(1, WO, f"ldwo_s{s}")
        weT = g3[0:8, 5696:5696 + 2048]

        class _Rec:
            def __init__(self, lst):
                self.lst = lst
            def op(self, *a, **k):
                self.lst.append((a, k))

        for blk in range(2):
            for tcl in range(8):
                tc = blk * 8 + tcl
                gb = 4 + (tc % 2)
                for kc in range(8):
                    P.op("tensor", lambda tc=tc, kc=kc, gb=gb: nc.tensor.matmul(
                            bank(gb, w=72), kreg[:, kc * 2048 + tc * 128:kc * 2048 + (tc + 1) * 128],
                            g3[:, 4096 + kc * 72:4096 + (kc + 1) * 72],
                            start=(kc == 0), stop=(kc == 7)),
                         reads=[f"vf{kc}", f"wgp{kc}"], writes=[f"pb{gb}"],
                         tag=f"gmm{tc}_{kc}")
                gp_ = tcl * 128
                gev = gwork[:, gp_:gp_ + 72]
                P.op("vector", lambda gb=gb, gev=gev: nc.vector.tensor_copy(gev, bank(gb, w=72)),
                     reads=[f"pb{gb}"], writes=[f"g_ev{tcl}"], tag=f"gev{tc}")
            GB = [[] for _ in range(8)]
            for tcl in range(8):
                tc = blk * 8 + tcl
                PT = _Rec(GB[tcl])
                gp_ = tcl * 128
                gev = gwork[:, gp_:gp_ + 72]
                r_ = gwork[:, gp_ + 72:gp_ + 80]
                PT.op("vector", lambda gev=gev, r_=r_: nc.vector.tensor_tensor(
                        r_, gev[:, 0:8], bgb[:, :], ALU.add),
                      reads=[f"g_ev{tcl}", "bgb"], writes=[f"g_r{tcl}"], tag=f"gr{tc}")
                pv2 = gev[:, 8:72]
                PT.op("vector", lambda pv2=pv2: nc.vector.scalar_tensor_tensor(
                        pv2, pv2, 0.0, pv2, ALU.add, ALU.mult),
                      reads=[f"g_ev{tcl}"], writes=[f"g_ev{tcl}"], tag=f"gpv2{tc}")
                c2 = gwork[:, gp_ + 80:gp_ + 88]
                PT.op("vector", lambda pv2=pv2, c2=c2: nc.vector.tensor_add(
                        c2, pv2[:, 0:8], pv2[:, 8:16]), reads=[f"g_ev{tcl}"],
                      writes=[f"g_c2{tcl}"])
                for pstep in range(2, 8):
                    PT.op("vector", lambda pv2=pv2, c2=c2, pstep=pstep: nc.vector.tensor_add(
                            c2, c2, pv2[:, pstep * 8:(pstep + 1) * 8]),
                          reads=[f"g_ev{tcl}", f"g_c2{tcl}"], writes=[f"g_c2{tcl}"])

                def gln(srck, src, g, gk, b, bk, outk, out):
                    glb = 10 + tcl * 3
                    mean = lnw[:, glb:glb + 1]; ss2 = lnw[:, glb + 1:glb + 2]
                    rst = lnw[:, glb + 2:glb + 3]
                    rc = gwork[:, gp_:gp_ + 8] if outk.startswith("g_lnr") else gwork[:, gp_ + 16:gp_ + 24]
                    scr = gwork[:, gp_ + 8:gp_ + 16]
                    PT.op("vector", lambda src=src, mean=mean: nc.vector.reduce_sum(mean, src, AX.X),
                          reads=[srck], writes=[f"g_mean{tcl}"])
                    PT.op("vector", lambda mean=mean: nc.vector.tensor_scalar_mul(mean, mean, 0.125),
                          reads=[f"g_mean{tcl}"], writes=[f"g_mean{tcl}"])
                    PT.op("vector", lambda src=src, rc=rc, mean=mean: nc.vector.tensor_scalar(
                            rc, src, mean, None, ALU.subtract),
                          reads=[srck], sreads=[f"g_mean{tcl}"], writes=[outk + "_rc", f"g_ev{tcl}"])
                    PT.op("vector", lambda rc=rc, scr=scr: nc.vector.scalar_tensor_tensor(
                            scr, rc, 0.0, rc, ALU.add, ALU.mult),
                          reads=[outk + "_rc"], writes=[f"g_scr{tcl}", f"g_ev{tcl}"])
                    PT.op("vector", lambda scr=scr, ss2=ss2: nc.vector.reduce_sum(ss2, scr, AX.X),
                          reads=[f"g_scr{tcl}"], writes=[f"g_ss2{tcl}"])
                    PT.op("scalar", lambda ss2=ss2, rst=rst: nc.scalar.activation(
                            rst, ss2, AF.Ln, bias=kc_eps, scale=kc_8),
                          reads=[f"g_ss2{tcl}", "konst"], writes=[f"g_rst{tcl}"])
                    PT.op("scalar", lambda rst=rst: nc.scalar.activation(
                            rst, rst, AF.Exp, scale=kc_nh),
                          sreads=[f"g_rst{tcl}"], reads=["konst"], writes=[f"g_rst{tcl}"])
                    PT.op("vector", lambda rc=rc, rst=rst, out=out: nc.vector.tensor_scalar(
                            out, rc, rst, None, ALU.mult),
                          reads=[outk + "_rc"], sreads=[f"g_rst{tcl}"], writes=[outk])
                    PT.op("vector", lambda out=out, g=g: nc.vector.tensor_mul(out, out, g[:, :]),
                          reads=[outk, gk], writes=[outk])
                    PT.op("vector", lambda out=out, b=b: nc.vector.tensor_add(out, out, b[:, :]),
                          reads=[outk, bk], writes=[outk])

                lnr = gwork[:, gp_ + 96:gp_ + 104]; lnc = gwork[:, gp_ + 104:gp_ + 112]
                gln(f"g_r{tcl}", r_, grb, "grb", brb, "brb", f"g_lnr{tcl}", lnr)
                cval = gwork[:, gp_ + 88:gp_ + 96]
                PT.op("scalar", lambda c2=c2, cval=cval: nc.scalar.activation(
                        cval, c2, AF.Ln, scale=kc_8),
                      reads=[f"g_c2{tcl}", "konst"], writes=[f"g_cval{tcl}"], tag=f"gcln{tc}")
                PT.op("scalar", lambda cval=cval: nc.scalar.activation(
                        cval, cval, AF.Exp, scale=kc_ph),
                      sreads=[f"g_cval{tcl}"], reads=["konst"], writes=[f"g_cval{tcl}"],
                      tag=f"gcexp{tc}")
                gln(f"g_cval{tcl}", cval, gcb, "gcb", bcb, "bcb", f"g_lnc{tcl}", lnc)
                sgs = sgat[:, tcl * 8:(tcl + 1) * 8]
                PT.op("vector", lambda lnr=lnr, lnc=lnc, sgs=sgs: nc.vector.scalar_tensor_tensor(
                        sgs, lnc, sigb[:, :], lnr, ALU.mult, ALU.add),
                      reads=[f"g_lnr{tcl}", f"g_lnc{tcl}"], sreads=["sigb"], writes=[f"g_s{tcl}"],
                      tag=f"gcomb{tc}")
                mx8 = gwork[:, gp_ + 24:gp_ + 32]; mxp = gwork[:, gp_ + 32:gp_ + 40]
                PT.op("vector", lambda mx8=mx8, sgs=sgs: nc.vector.max(out=mx8, in_=sgs),
                      reads=[f"g_s{tcl}"], writes=[f"g_mx8{tcl}", f"g_ev{tcl}"], tag=f"gmax{tc}")
                PT.op("vector", lambda mxp=mxp: nc.vector.memset(mxp, 1e38),
                      writes=[f"g_mxp{tcl}", f"g_ev{tcl}"])
                PT.op("vector", lambda mxp=mxp, mx8=mx8: nc.vector.tensor_copy(mxp[:, 0:1], mx8[:, 0:1]),
                      reads=[f"g_mx8{tcl}"], writes=[f"g_mxp{tcl}"])
                srep = gwork[:, gp_ + 40:gp_ + 48]
                PT.op("vector", lambda srep=srep, mxp=mxp, sgs=sgs: nc.vector.match_replace(
                        out=srep, in_to_replace=mxp, in_values=sgs, imm_value=-3e38),
                      reads=[f"g_mxp{tcl}", f"g_s{tcl}"], writes=[f"g_srep{tcl}", f"g_ev{tcl}"],
                      tag=f"gmr1{tc}")
                PT.op("vector", lambda srep=srep, tc=tc: nc.vector.tensor_single_scalar(
                        eq1a[:, tc * 8:(tc + 1) * 8], srep, -1e38, ALU.is_le),
                      reads=[f"g_srep{tcl}"], writes=[f"eq1_{tc}"], tag=f"geq1{tc}")
                PT.op("vector", lambda mxp=mxp: nc.vector.memset(mxp, 1e38),
                      reads=[f"g_srep{tcl}"], writes=[f"g_mxp{tcl}"])
                PT.op("vector", lambda mxp=mxp, mx8=mx8: nc.vector.tensor_copy(mxp[:, 0:1], mx8[:, 1:2]),
                      reads=[f"g_mx8{tcl}"], writes=[f"g_mxp{tcl}"])
                srep2 = gwork[:, gp_ + 48:gp_ + 56]
                PT.op("vector", lambda srep=srep, srep2=srep2, mxp=mxp: nc.vector.match_replace(
                        out=srep2, in_to_replace=mxp, in_values=srep, imm_value=-3e38),
                      reads=[f"g_mxp{tcl}", f"g_srep{tcl}"], writes=[f"g_srep2{tcl}", f"g_ev{tcl}"],
                      tag=f"gmr2{tc}")
                PT.op("vector", lambda srep2=srep2, tc=tc: nc.vector.tensor_single_scalar(
                        eq2a[:, tc * 8:(tc + 1) * 8], srep2, -1e38, ALU.is_le),
                      reads=[f"g_srep2{tcl}"], writes=[f"eq2_{tc}"], tag=f"geq2{tc}")
                PT.op("vector", lambda tc=tc: nc.vector.tensor_sub(
                        eq2a[:, tc * 8:(tc + 1) * 8], eq2a[:, tc * 8:(tc + 1) * 8],
                        eq1a[:, tc * 8:(tc + 1) * 8]),
                      reads=[f"eq1_{tc}", f"eq2_{tc}"], writes=[f"eq2_{tc}"])
                PT.op("vector", lambda mx8=mx8, tc=tc: nc.vector.tensor_sub(
                        dda[:, tc:tc + 1], mx8[:, 0:1], mx8[:, 1:2]),
                      reads=[f"g_mx8{tcl}"], writes=[f"dd_{tc}"], tag=f"gdd{tc}")
            mx = max(len(g_) for g_ in GB)
            for st in range(mx):
                for tcl in range(8):
                    if st < len(GB[tcl]):
                        a, k = GB[tcl][st]
                        P.op(*a, **k)
        P.op("scalar", lambda: nc.scalar.activation(w1a[:, :], dda[:, :], AF.Sigmoid),
             reads=[f"dd_{tc}" for tc in range(NTC)], writes=["w1a"], tag="gsig1")
        P.op("scalar", lambda: nc.scalar.activation(w2a[:, :], dda[:, :], AF.Sigmoid,
                                                    scale=kc_m1),
             reads=[f"dd_{tc}" for tc in range(NTC)] + ["konst"], writes=["w2a"],
             tag="gsig2")
        for tc in range(NTC):
            gp_ = (tc % 8) * 128
            t1 = gwork[:, gp_ + 8:gp_ + 16]; t2 = gwork[:, gp_ + 16:gp_ + 24]
            P.op("vector", lambda tc=tc, t1=t1: nc.vector.tensor_scalar(
                    t1, eq1a[:, tc * 8:(tc + 1) * 8], w1a[:, tc:tc + 1], None, ALU.mult),
                 reads=[f"eq1_{tc}"], sreads=["w1a"], writes=[f"g_t1{tc % 8}"], tag=f"gwe1{tc}")
            P.op("vector", lambda tc=tc, t2=t2: nc.vector.tensor_scalar(
                    t2, eq2a[:, tc * 8:(tc + 1) * 8], w2a[:, tc:tc + 1], None, ALU.mult),
                 reads=[f"eq2_{tc}"], sreads=["w2a"], writes=[f"g_t2{tc % 8}"], tag=f"gwe2{tc}")
            web16 = web_s[:, (tc % 8) * 8:(tc % 8) * 8 + 8]
            P.op("vector", lambda t1=t1, t2=t2, web16=web16: nc.vector.tensor_tensor(
                    web16, t1, t2, ALU.add),
                 reads=[f"g_t1{tc % 8}", f"g_t2{tc % 8}"], writes=[f"g_web{tc % 8}"], tag=f"gwe{tc}")
            tb = 6 + (tc % 2)
            P.op("tensor", lambda web16=web16, tb=tb: nc.tensor.transpose(
                    bbank(tb, r1=8, w=128)[0:8, 0:128], web16, identb[:, :]),
                 reads=[f"g_web{tc % 8}", "identb"], writes=[f"pb{tb}"], tag=f"gwtr{tc}")
            P.op("vector", lambda tc=tc, tb=tb: nc.vector.tensor_copy(
                    weT[:, tc * 128:(tc + 1) * 128], bbank(tb, r1=8, w=128)[0:8, 0:128]),
                 reads=[f"pb{tb}"], writes=[f"weT{tc // 4}"], tag=f"gwev{tc}")

        P.op("vector", lambda: nc.vector.drain(),
             reads=[f"weT{j}" for j in range(4)], writes=["weTdrain"], tag="wetdrain")
        for hc2 in range(2):
            for nt in range(NNT):
                hb = (hc2 * NNT + nt) % 2
                wb = 2 + hb
                P.op("tensor", lambda hc2=hc2, nt=nt, wb=wb: nc.tensor.matmul(
                        bank(wb), expand[:, hc2 * 128:(hc2 + 1) * 128],
                        weT[:, nt * 512:(nt + 1) * 512], start=True, stop=True),
                     reads=[f"weT{nt}", "expand", "weTdrain"], writes=[f"pb{wb}"],
                     tag=f"wex{hc2}_{nt}")
                for kc in range(8):
                    P.op("tensor", lambda hc2=hc2, nt=nt, kc=kc, hb=hb: nc.tensor.matmul(
                            bank(hb), g3[:, kc * 256 + hc2 * 128:kc * 256 + (hc2 + 1) * 128],
                            kreg[:, kc * 2048 + nt * 512:kc * 2048 + (nt + 1) * 512],
                            start=(kc == 0), stop=(kc == 7)),
                         reads=[f"w1t{kc}", f"vf{kc}"], writes=[f"pb{hb}"],
                         tag=f"hmm{hc2}_{nt}_{kc}")
                sil = silst[:, hb * 512:(hb + 1) * 512]
                P.op("scalar", lambda hb=hb, hc2=hc2, sil=sil: nc.scalar.activation(
                        sil, bank(hb), AF.Silu, bias=b1_s[:, hc2:hc2 + 1]),
                     reads=[f"pb{hb}", "b1"], writes=[f"sil{hb}"],
                     tag=f"silu{hc2}_{nt}")
                P.op("vector", lambda hc2=hc2, nt=nt, wb=wb, sil=sil: nc.vector.tensor_tensor(
                        g1[:, hc2 * 2048 + nt * 512:hc2 * 2048 + (nt + 1) * 512],
                        bank(wb), sil, ALU.mult),
                     reads=[f"pb{wb}", f"sil{hb}"], writes=[f"hs{hc2}n{nt}"],
                     tag=f"hscale{hc2}_{nt}")
        for mc in range(8):
            for nt in range(NNT):
                db = (mc * NNT + nt) % 2
                P.op("tensor", lambda mc=mc, nt=nt, db=db: nc.tensor.matmul(
                        bank(db), g3[0:8, 4672 + mc * 128:4672 + (mc + 1) * 128],
                        weT[:, nt * 512:(nt + 1) * 512], start=True, stop=False),
                     reads=["b2m", f"weT{nt}"], writes=[f"pb{db}"],
                     tag=f"db2{mc}_{nt}")
                for hc2 in range(2):
                    P.op("tensor", lambda mc=mc, nt=nt, hc2=hc2, db=db: nc.tensor.matmul(
                            bank(db), g3[:, 2048 + hc2 * 1024 + mc * 128:2048 + hc2 * 1024 + (mc + 1) * 128],
                            g1[:, hc2 * 2048 + nt * 512:hc2 * 2048 + (nt + 1) * 512],
                            start=False, stop=(hc2 == 1)),
                         reads=[f"w2t{hc2}", f"hs{hc2}n{nt}"], writes=[f"pb{db}"],
                         tag=f"dmm{mc}_{nt}_{hc2}")
                P.op("vector", lambda mc=mc, nt=nt, db=db: nc.vector.tensor_tensor(
                        xnf[:, mc * 2048 + nt * 512:mc * 2048 + (nt + 1) * 512],
                        bank(db), kreg[:, mc * 2048 + nt * 512:mc * 2048 + (nt + 1) * 512],
                        ALU.add),
                     reads=[f"pb{db}", f"vf{mc}"], writes=[f"ffn{mc}n{nt}"],
                     tag=f"dev{mc}_{nt}")
        P.barrier()

        for tc in range(NTC):
            xre = kf[:, (tc % 2) * 1024:(tc % 2) * 1024 + 1024]
            P.op("sync", lambda s=s, tc=tc, xre=xre: nc.sync.dma_start(
                    xre, x_d[s, tc * 128:(tc + 1) * 128, :]),
                 writes=[f"xre{tc % 2}"], dma=True, tag=f"ldxre{s}_{tc}", sem=f"xre{tc % 2}")
            tb = 6 + (tc % 2)
            for mc in range(8):
                P.op("tensor", lambda mc=mc, tc=tc, tb=tb: nc.tensor.transpose(
                        bbank(tb, c0=mc * 128, w=128),
                        xnf[:, mc * 2048 + tc * 128:mc * 2048 + (tc + 1) * 128],
                        identb[:, :]),
                     reads=[f"ffn{mc}n{tc // 4}", "identb"], writes=[f"pb{tb}"],
                     tag=f"ftr{tc}_{mc}")
            for nt2 in range(2):
                ab = (tc * 2 + nt2) % 4
                for kc in range(8):
                    P.op("tensor", lambda tc=tc, nt2=nt2, kc=kc, ab=ab: nc.tensor.matmul(
                            bank(ab), qctx[:, kc * 2048 + tc * 128:kc * 2048 + (tc + 1) * 128],
                            wslot(1)[:, kc * 1024 + nt2 * 512:kc * 1024 + (nt2 + 1) * 512],
                            start=(kc == 0), stop=(kc == 7)),
                         reads=[f"w1c{kc}", f"q{kc}n{tc // 4}"], writes=[f"pb{ab}"],
                         tag=f"womm{tc}_{nt2}_{kc}")
                ob = kf[:, 2048 + (tc % 2) * 1024 + nt2 * 512:2048 + (tc % 2) * 1024 + (nt2 + 1) * 512]
                P.op("vector", lambda ob=ob, ab=ab, xre=xre, nt2=nt2: nc.vector.tensor_tensor(
                        ob, bank(ab), xre[:, nt2 * 512:(nt2 + 1) * 512], ALU.add),
                     reads=[f"pb{ab}", f"xre{tc % 2}"], writes=[f"ob{tc % 2}_{nt2}"],
                     tag=f"asm1_{tc}_{nt2}")
                P.op("vector", lambda ob=ob, tb=tb, nt2=nt2: nc.vector.tensor_tensor(
                        ob, bbank(tb, c0=nt2 * 512, w=512), ob, ALU.add),
                     reads=[f"pb{tb}", f"ob{tc % 2}_{nt2}"], writes=[f"ob{tc % 2}_{nt2}"],
                     tag=f"asm2_{tc}_{nt2}")
            P.op("sync", lambda s=s, tc=tc: nc.sync.dma_start(
                    out_d[s, tc * 128:(tc + 1) * 128, :],
                    kf[:, 2048 + (tc % 2) * 1024:2048 + (tc % 2) * 1024 + 1024]),
                 reads=[f"ob{tc % 2}_0", f"ob{tc % 2}_1"], dma=True,
                 tag=f"stout{s}_{tc}", sem=f"outst{tc % 2}")
        if s == 0:
            wload(0, WQ, "ldwq_s1")
        P.barrier()

    P.emit()
    return nc


def kernel(**inputs):
    x = np.ascontiguousarray(inputs["L0_attn_hidden_states"])
    wq = np.ascontiguousarray(inputs["L0_attn_Wq"])
    wk = np.ascontiguousarray(inputs["L0_attn_Wk"])
    wv = np.ascontiguousarray(inputs["L0_attn_Wv"])
    wo = np.ascontiguousarray(inputs["L0_attn_Wo"])
    relb = np.ascontiguousarray(inputs["L0_attn_rel_bias"])
    wvf = np.ascontiguousarray(inputs["L0_ffn_Wv"].T)
    bv = np.ascontiguousarray(inputs["L0_ffn_bv"])
    wg = inputs["L0_ffn_Wg"]
    bg = inputs["L0_ffn_bg"]
    wp = inputs["L0_ffn_Wp"]
    alpha = np.asarray(inputs["L0_ffn_alpha"]).reshape(())
    g_r = inputs["L0_ffn_g_r"]; b_r = inputs["L0_ffn_b_r"]
    g_c = inputs["L0_ffn_g_c"]; b_c = inputs["L0_ffn_b_c"]
    w1 = inputs["L0_ffn_W1"]
    b1 = inputs["L0_ffn_b1"]
    w2 = inputs["L0_ffn_W2"]
    b2 = np.ascontiguousarray(inputs["L0_ffn_b2"])

    wgp = np.concatenate([np.asarray(wg).T,
                          np.asarray(wp).transpose(2, 1, 0).reshape(1024, 64)],
                         axis=1).astype(np.float32)
    w1t = np.asarray(w1).transpose(2, 0, 1).reshape(1024, 256).astype(np.float32)
    w2t = np.asarray(w2).transpose(0, 2, 1).reshape(256, 1024).astype(np.float32)
    bvr = np.asarray(bv).reshape(8, 128).T.astype(np.float32)
    b1r = np.asarray(b1).reshape(256).reshape(2, 128).T.astype(np.float32)
    gp = np.zeros((1, 48), np.float32)
    gp[0, 0:8] = g_r; gp[0, 8:16] = b_r; gp[0, 16:24] = g_c; gp[0, 24:32] = b_c
    gp[0, 32:40] = bg; gp[0, 40] = alpha

    buckets = _bucket_table()
    t_idx = np.arange(4095)
    bt = buckets[(2047 - t_idx) + 2047]
    import ml_dtypes
    ohrev = np.zeros((32, 4095), dtype=ml_dtypes.bfloat16)
    ohrev[bt, t_idx] = 1
    identb = np.eye(128, dtype=ml_dtypes.bfloat16)
    identf = np.eye(128, dtype=np.float32)
    onesr = np.ones((1, 128), np.float32)
    expand = np.zeros((8, 256), dtype=ml_dtypes.bfloat16)
    for e in range(8):
        expand[e, e * 32:(e + 1) * 32] = 1
    parexp = np.zeros((128, 128), dtype=ml_dtypes.bfloat16)
    for c in range(8):
        for p in range(128):
            parexp[p, c * 16 + 2 * c + (1 if p >= 64 else 0)] = 1

    nc = build_program()

    common = dict(wq=wq, wk=wk, wv=wv, wo=wo, wvf=np.ascontiguousarray(wvf),
                  wgp=wgp, w1t=w1t, w2t=w2t, b2m=b2, bvr=bvr, b1r=b1r, gp=gp,
                  relb=relb, ohrev=ohrev, identb=identb, identf=identf,
                  onesr=onesr, expand=np.asarray(expand), parexp=np.asarray(parexp))
    in_maps = []
    for core in range(8):
        m = dict(common)
        m["x"] = np.ascontiguousarray(x[core * 2:(core + 1) * 2])
        in_maps.append(m)
    res = run_bass_kernel_spmd(nc, in_maps, core_ids=list(range(8)))
    kernel.last_results = res
    return np.concatenate([r["out"] for r in res.results], axis=0)


# version 08 (final)
# speedup vs baseline: 1.0549x; 1.0170x over previous
import dataclasses
import numpy as np
import concourse.bass as bass
import concourse.mybir as mybir
from concourse.bass_utils import run_bass_kernel_spmd

F32 = mybir.dt.float32
BF16 = mybir.dt.bfloat16
AF = mybir.ActivationFunctionType
ALU = mybir.AluOpType
AX = mybir.AxisListType

ENGINES = ("sync", "tensor", "vector", "scalar", "gpsimd")
TRUST_SAME_ENGINE = {"tensor"}

S_LEN = 2048
D = 1024
NH = 16
DK = 64
NCHUNK = 8
NTC = 16
NNT = 4
VST = 66
T_VSBT = 3968
A_VSBT = 127


class Prog:
    def __init__(self, nc):
        self.nc = nc
        self.ops = []
        self.buf = {}
        self.bar = {e: -1 for e in ENGINES}

    def _st(self, b):
        if b not in self.buf:
            self.buf[b] = {"writer": None, "readers": []}
        return self.buf[b]

    def op(self, engine, emit, reads=(), writes=(), sreads=(), dma=False, tag="",
           nobarrier=False, sem=None, extra_deps=()):
        idx = len(self.ops)
        deps = set(extra_deps)
        sdeps = set()
        for b in reads:
            st = self._st(b)
            if st["writer"] is not None:
                deps.add(st["writer"])
        for b in sreads:
            st = self._st(b)
            if st["writer"] is not None:
                deps.add(st["writer"])
                sdeps.add(st["writer"])
        for b in writes:
            st = self._st(b)
            if st["writer"] is not None:
                deps.add(st["writer"])
            deps.update(st["readers"])
        for e, last in self.bar.items():
            if last >= 0:
                deps.add(last)
        self.ops.append(dict(idx=idx, engine=engine, emit=emit, deps=deps,
                             sdeps=sdeps, dma=dma, tag=tag, sem=(sem or engine)))
        for b in reads:
            self._st(b)["readers"].append(idx)
        for b in sreads:
            self._st(b)["readers"].append(idx)
        for b in writes:
            self.buf[b] = {"writer": idx, "readers": []}
        if not nobarrier:
            self._last_nb = getattr(self, "_last_nb", {e: -1 for e in ENGINES})
            self._last_nb[engine] = idx
        return idx

    def barrier(self, settle=None):
        last = getattr(self, "_last_nb", {e: -1 for e in ENGINES})
        if settle is not None:
            idx = self.op("sync", settle, extra_deps=[i for i in last.values() if i >= 0],
                          writes=[f"__settle{len(self.ops)}"], dma=True, sem="s_settle")
            last = dict(getattr(self, "_last_nb"))
        self.bar = dict(last)

    def emit(self):
        nc = self.nc
        n = len(self.ops)

        def will_wait(d, o):
            de = self.ops[d]["engine"]
            if de != o["engine"]:
                return True
            if de in TRUST_SAME_ENGINE and d not in o["sdeps"]:
                return False
            return True

        need_inc = [False] * n
        for o in self.ops:
            if o["dma"]:
                need_inc[o["idx"]] = True
            for d in o["deps"]:
                if will_wait(d, o):
                    need_inc[d] = True
        semnames = []
        for o in self.ops:
            if o["sem"] not in semnames:
                semnames.append(o["sem"])
        count_after = {}
        totals = {sn: 0 for sn in semnames}
        for o in self.ops:
            if need_inc[o["idx"]]:
                totals[o["sem"]] += 16 if o["dma"] else 1
            count_after[o["idx"]] = totals[o["sem"]]
        waits = []
        for o in self.ops:
            w = {}
            for d in o["deps"]:
                if not will_wait(d, o):
                    continue
                dsn = self.ops[d]["sem"]
                c = count_after[d]
                if c > 0:
                    w[dsn] = max(w.get(dsn, 0), c)
            waits.append(w)
        low = {e: {sn: 0 for sn in semnames} for e in ENGINES}
        pruned = []
        for o in self.ops:
            e = o["engine"]
            w2 = []
            for se, v in sorted(waits[o["idx"]].items()):
                if low[e][se] < v:
                    w2.append((se, v))
                    low[e][se] = v
            pruned.append(w2)

        sems = {sn: nc.alloc_semaphore(f"sem_{sn}") for sn in semnames}
        per_engine = {e: [] for e in ENGINES}
        for o in self.ops:
            per_engine[o["engine"]].append(o)

        def runner(eng_name):
            def run(eng):
                for o in per_engine[eng_name]:
                    for (se, v) in pruned[o["idx"]]:
                        eng.wait_ge(sems[se], v)
                    ins = o["emit"]()
                    if need_inc[o["idx"]]:
                        amt = 16 if o["dma"] else 1
                        ins.then_inc(sems[o["sem"]], amt)
                if eng_name == "sync":
                    for sn in semnames:
                        if totals[sn] > 0:
                            eng.wait_ge(sems[sn], totals[sn])
            return run

        with nc.Block() as block:
            block.sync(runner("sync"))
            block.tensor(runner("tensor"))
            block.vector(runner("vector"))
            block.scalar(runner("scalar"))
            block.gpsimd(runner("gpsimd"))


def _bucket_table():
    nb = 16
    max_exact = 8
    delta = np.arange(-2047, 2048, dtype=np.int64)
    buckets = (delta > 0).astype(np.int64) * nb
    rp = np.abs(delta)
    is_small = rp < max_exact
    log_ratio = (np.log(np.maximum(rp, 1).astype(np.float32) / max_exact)
                 / np.float32(np.log(128.0 / max_exact)))
    large = max_exact + (log_ratio * (nb - max_exact)).astype(np.int32)
    large = np.minimum(large, nb - 1)
    return (buckets + np.where(is_small, rp, large)).astype(np.int64)


def build_program():
    nc = bass.Bass("TRN2", target_bir_lowering=False)
    din = lambda name, shape, dt=F32: nc.dram_tensor(name, shape, dt, kind="ExternalInput").ap()

    x_d = din("x", [2, S_LEN, D])
    wq_d = din("wq", [D, D]);  wk_d = din("wk", [D, D])
    wv_d = din("wv", [D, D]);  wo_d = din("wo", [D, D])
    wvf_d = din("wvf", [D, D])
    wgp_d = din("wgp", [D, 72])
    w1t_d = din("w1t", [D, 256]); w2t_d = din("w2t", [256, D])
    b2m_d = din("b2m", [8, D])
    bvr_d = din("bvr", [128, 8]); b1r_d = din("b1r", [128, 2])
    gp_d = din("gp", [1, 48])
    relb_d = din("relb", [32, 16])
    ohrev_d = din("ohrev", [32, 4095], BF16)
    identb_d = din("identb", [128, 128], BF16)
    identf_d = din("identf", [128, 128])
    onesr_d = din("onesr", [1, 128])
    expand_d = din("expand", [8, 256], BF16)
    parexp_d = din("parexp", [128, 128], BF16)

    out_d = nc.dram_tensor("out", [2, S_LEN, D], F32, kind="ExternalOutput").ap()
    vsbt_d = nc.dram_tensor("vsbt_dram", [16, 128, T_VSBT], BF16, kind="Internal").ap()
    vf_d = nc.dram_tensor("vf_dram", [8, 128, 2048], BF16, kind="Internal").ap()
    nsh_d = nc.dram_tensor("nsh_dram", [2, 16], F32, kind="Internal").ap()
    settle_d = nc.dram_tensor("settle_dram", [16], F32, kind="Internal").ap()

    P = Prog(nc)
    SB = lambda name, shape, dt: nc.alloc_sbuf_tensor(name, shape, dt)

    xnf = SB("xnf", [128, 16384], BF16)
    qctx = SB("qctx", [128, 16384], BF16)
    kreg = SB("kreg", [128, 16384], BF16)
    vtok = SB("vtok", [128, 16 * 1056], BF16)
    g1 = SB("g1", [128, 24576], BF16)
    g3 = SB("g3", [128, 8192], BF16)

    negsh = SB("negsh", [128, 16], F32)
    konst = SB("konst", [128, 8], F32)
    bv_s = SB("bv_s", [128, 8], F32)
    b1_s = SB("b1_s", [128, 2], F32)
    grb = SB("grb", [128, 8], F32); brb = SB("brb", [128, 8], F32)
    gcb = SB("gcb", [128, 8], F32); bcb = SB("bcb", [128, 8], F32)
    bgb = SB("bgb", [128, 8], F32)
    sigb = SB("sigb", [128, 1], F32)
    identb = SB("identb_s", [128, 128], BF16)
    identf = SB("identf_s", [128, 128], F32)
    onesr = SB("onesr_s", [1, 128], F32)
    expand = SB("expand_s", [8, 256], BF16)
    parexp = SB("parexp_s", [128, 128], BF16)
    relb = SB("relb_s", [32, 16], BF16)
    gp_s = SB("gp_s", [1, 48], F32)
    vstage = SB("vstage", [128, 1024], BF16)
    ctxst = SB("ctxst", [128, 520], BF16)
    silst = SB("silst", [128, 1024], BF16)
    sgat = SB("sgat", [128, 64], F32)
    gwork = SB("gwork", [128, 1024], F32)
    eq1a = SB("eq1a", [128, 128], F32)
    eq2a = SB("eq2a", [128, 128], F32)
    dda = SB("dda", [128, 16], F32)
    w1a = SB("w1a", [128, 16], F32)
    w2a = SB("w2a", [128, 16], F32)
    lnw = SB("lnw", [128, 40], F32)
    zc = SB("zc", [128, 8], F32)
    m16 = SB("m16", [16, 16], F32)
    r16 = SB("r16", [1, 32], F32)
    web_s = SB("web_s", [128, 64], BF16)

    psall = nc.alloc_psum_tensor("psall", [128, 4096], F32)
    psap = psall.ap()
    psb = psap.bitcast(BF16)

    def bank(i, r0=0, r1=128, c0=0, w=512):
        return psall[r0:r1, i * 512 + c0:i * 512 + c0 + w]

    def wide(i, c0=0, w=1024):
        return psall[:, i * 1024 + c0:i * 1024 + c0 + w]

    def bbank(i, r0=0, r1=128, c0=0, w=1024):
        return psb[r0:r1, i * 1024 + c0:i * 1024 + c0 + w]

    g1f = g1.ap().bitcast(F32)
    g3f = g3.ap().bitcast(F32)
    kf = kreg.ap().bitcast(F32)

    def settle_emit():
        return nc.sync.dma_start(settle_d[0:16], konst[0:1, 0:8].rearrange("a b -> (a b)")
                                 if False else m16[0:16, 0:1])

    KV = {}
    def kcol(v):
        if v not in KV:
            i = len(KV)
            KV[v] = i
            P.op("vector", lambda v=v, i=i: nc.vector.memset(konst[:, i:i + 1], v),
                 writes=["konst"], tag=f"konst{i}")
        return konst[:, KV[v]:KV[v] + 1]

    def ld(engine, dst, src, key, tag):
        P.op(engine, lambda: getattr(nc, engine).dma_start(dst, src),
             writes=[key], dma=True, tag=tag, sem="s_" + key)

    ld("sync", identf[:, :], identf_d[:, :], "identf", "ld_identf")
    ld("sync", onesr[:, :], onesr_d[:, :], "onesr", "ld_onesr")
    ld("sync", gp_s[:, :], gp_d[:, :], "gp", "ld_gp")
    ld("sync", bv_s[:, :], bvr_d[:, :], "bv", "ld_bv")
    ld("sync", b1_s[:, :], b1r_d[:, :], "b1", "ld_b1")
    ld("sync", identb[:, :], identb_d[:, :], "identb", "ld_identb")
    ld("sync", expand[:, :], expand_d[:, :], "expand", "ld_expand")
    ld("sync", parexp[:, :], parexp_d[:, :], "parexp", "ld_parexp")
    ld("gpsimd", relb[:, :], relb_d[:, :], "relb", "ld_relb")
    ohrev_v = g1[0:32, 16384:16384 + 4095]
    ld("sync", ohrev_v, ohrev_d[:, :], "ohrev", "ld_ohrev")
    vrev_v = g1[0:16, 20480:20480 + 4095]

    kc_eps = kcol(1e-5); kc_8 = kcol(0.125); kc_nh = kcol(-0.5)
    kc_ph = kcol(0.5); kc_m1 = kcol(-1.0)

    P.op("tensor", lambda: nc.tensor.matmul(bank(6, c0=0, w=40)[:, 0:40], onesr[0:1, :],
                                            gp_s[0:1, 0:40], start=True, stop=True),
         reads=["onesr", "gp"], writes=["pb6"], tag="gpbcast")
    for t, key, c0 in [(grb, "grb", 0), (brb, "brb", 8), (gcb, "gcb", 16),
                       (bcb, "bcb", 24), (bgb, "bgb", 32)]:
        P.op("vector", lambda t=t, c0=c0: nc.vector.tensor_copy(t[:, :], bank(6)[:, c0:c0 + 8]),
             reads=["pb6"], writes=[key], tag=f"bc_{key}")
    siga = SB("siga", [1, 1], F32)
    P.op("scalar", lambda: nc.scalar.activation(siga[:, :], gp_s[0:1, 40:41], AF.Sigmoid),
         reads=["gp"], writes=["siga"], tag="sigalpha")
    P.op("scalar", lambda: nc.scalar.drain(), reads=["siga"], writes=["sigadrain"],
         tag="sigadrain")
    P.op("tensor", lambda: nc.tensor.matmul(bank(7, c0=0, w=1)[:, 0:1], onesr[0:1, :],
                                            siga[0:1, :], start=True, stop=True),
         reads=["onesr", "siga", "sigadrain"], writes=["pb7"], tag="sigbcast")
    P.op("vector", lambda: nc.vector.tensor_copy(sigb[:, :], bank(7)[:, 0:1]),
         reads=["pb7"], writes=["sigb"], tag="sigevac")

    for nt in range(8):
        w = 512 if nt < 7 else 511
        b = 4 + (nt % 2)
        P.op("tensor", lambda nt=nt, w=w, b=b: nc.tensor.matmul(
                bank(b, r1=16, w=w), relb[:, :], ohrev_v[:, nt * 512:nt * 512 + w],
                start=True, stop=True),
             reads=["relb", "ohrev"], writes=[f"pb{b}"], tag=f"vrev{nt}")
        P.op("scalar", lambda nt=nt, w=w, b=b: nc.scalar.activation(
                vrev_v[:, nt * 512:nt * 512 + w], bank(b, r1=16, w=w), AF.Exp),
             reads=[f"pb{b}"], writes=["vrev"], tag=f"vrevexp{nt}")
    for j in range(4):
        src = dataclasses.replace(
            g1.ap(), offset=20480 + A_VSBT - 32 * j,
            ap=[[24576, 16], [-1, 32], [1, T_VSBT]])
        P.op("gpsimd", lambda j=j, src=src: nc.gpsimd.dma_start(
                vsbt_d[:, 32 * j:32 * (j + 1), :], src),
             reads=["vrev"], writes=[f"vsbtd{j}"], dma=True, tag=f"vsbtbuild{j}",
             sem="vsbtb")

    WQ, WK, WV, WVF, WO = wq_d, wk_d, wv_d, wvf_d, wo_d

    def wslot(s):
        return g1[:, s * 8192:(s + 1) * 8192]

    def wload(slot, wdram, tag):
        dst = wslot(slot)
        for c in range(8):
            P.op("gpsimd", lambda c=c, dst=dst, wdram=wdram: nc.gpsimd.dma_start(
                    dst[:, c * 1024:(c + 1) * 1024],
                    wdram[c * 128:(c + 1) * 128, :]),
                 reads=([f"w{slot}c{c - 2}"] if c >= 2 else []),
                 writes=[f"w{slot}c{c}"], dma=True, tag=f"{tag}c{c}", nobarrier=True,
                 sem=f"wl{slot}{c % 2}")

    def vsbt_load(s, h, tag):
        slot = h % 2
        dst = g1[:, 16384 + slot * T_VSBT:16384 + (slot + 1) * T_VSBT]
        P.op("sync", lambda dst=dst, h=h: nc.sync.dma_start(dst, vsbt_d[h, :, :]),
             reads=["vsbtd0", "vsbtd1", "vsbtd2", "vsbtd3"],
             writes=[f"vsbt{slot}"], dma=True, tag=tag, nobarrier=True, sem=f"vsbt{slot}")

    wload(0, WQ, "ldwq_s0")

    for s in range(2):
        def ln_stats(tc):
            xin = g3f[:, (tc % 2) * 1024:(tc % 2) * 1024 + 1024]
            xkey = f"xin{tc % 2}"
            P.op("sync", lambda s=s, tc=tc, xin=xin: nc.sync.dma_start(
                    xin, x_d[s, tc * 128:(tc + 1) * 128, :]),
                 writes=[xkey], dma=True, tag=f"ldx{s}_{tc}", sem=f"xin{tc % 2}")
            lb = (tc % 2) * 5
            sums = lnw[:, lb:lb + 1]; ssq = lnw[:, lb + 1:lb + 2]; mean = lnw[:, lb + 2:lb + 3]
            var = lnw[:, lb + 3:lb + 4]; rstd = lnw[:, lb + 4:lb + 5]
            scr = g3f[:, 3072:4096]
            P.op("vector", lambda xin=xin, sums=sums: nc.vector.reduce_sum(sums, xin, AX.X),
                 reads=[xkey], writes=[f"ln_sums{tc % 2}"], tag=f"lnsum{tc}")
            P.op("vector", lambda xin=xin, scr=scr: nc.vector.scalar_tensor_tensor(
                    scr, xin, 0.0, xin, ALU.add, ALU.mult),
                 reads=[xkey], writes=["lnscr"], tag=f"lnsq{tc}")
            P.op("vector", lambda scr=scr, ssq=ssq: nc.vector.reduce_sum(ssq, scr, AX.X),
                 reads=["lnscr"], writes=[f"ln_ssq{tc % 2}"], tag=f"lnssq{tc}")
            P.op("vector", lambda sums=sums, mean=mean: nc.vector.tensor_scalar_mul(
                    mean, sums, 1.0 / 1024), reads=[f"ln_sums{tc % 2}"], writes=[f"ln_mean{tc % 2}"])
            P.op("vector", lambda mean=mean, var=var: nc.vector.scalar_tensor_tensor(
                    var, mean, -1.0, mean, ALU.mult, ALU.mult),
                 reads=[f"ln_mean{tc % 2}"], writes=[f"ln_var{tc % 2}"])
            P.op("vector", lambda ssq=ssq, var=var: nc.vector.scalar_tensor_tensor(
                    var, ssq, 1.0 / 1024, var, ALU.mult, ALU.add),
                 reads=[f"ln_ssq{tc % 2}", f"ln_var{tc % 2}"], writes=[f"ln_var{tc % 2}"])
            P.op("scalar", lambda var=var, rstd=rstd: nc.scalar.activation(
                    rstd, var, AF.Ln, bias=kc_eps), reads=[f"ln_var{tc % 2}", "konst"],
                 writes=[f"ln_rstd{tc % 2}"], tag=f"lnln{tc}")
            P.op("scalar", lambda rstd=rstd: nc.scalar.activation(
                    rstd, rstd, AF.Exp, scale=kc_nh),
                 sreads=[f"ln_rstd{tc % 2}"], reads=["konst"], writes=[f"ln_rstd{tc % 2}"],
                 tag=f"lnexp{tc}")
            P.op("scalar", lambda: nc.scalar.drain(), reads=[f"ln_rstd{tc % 2}"],
                 writes=[f"lnrd{tc % 2}"], tag=f"lndrain{tc}")

        def ln_apply(tc):
            xin = g3f[:, (tc % 2) * 1024:(tc % 2) * 1024 + 1024]
            xkey = f"xin{tc % 2}"
            lb = (tc % 2) * 5
            mean = lnw[:, lb + 2:lb + 3]; rstd = lnw[:, lb + 4:lb + 5]
            xnt = g3[:, 4096 + (tc % 2) * 1024:4096 + (tc % 2) * 1024 + 1024]
            xntk = f"xnt{tc % 2}"
            P.op("vector", lambda xin=xin, xnt=xnt, mean=mean, rstd=rstd:
                    nc.vector.tensor_scalar(xnt, xin, mean, rstd, ALU.subtract, ALU.mult),
                 reads=[xkey, f"lnrd{tc % 2}"],
                 sreads=[f"ln_mean{tc % 2}", f"ln_rstd{tc % 2}"], writes=[xntk],
                 tag=f"lnapply{tc}")
            tb = 6 + (tc % 2)
            for c in range(8):
                P.op("tensor", lambda c=c, tb=tb, xnt=xnt: nc.tensor.transpose(
                        bbank(tb, c0=c * 128, w=128), xnt[:, c * 128:(c + 1) * 128],
                        identb[:, :]),
                     reads=[xntk, "identb"], writes=[f"pb{tb}"], tag=f"lntr{tc}_{c}")
            dst = dataclasses.replace(xnf.ap(), offset=tc * 128,
                                      ap=[[16384, 128], [2048, 8], [1, 128]])
            P.op("vector", lambda dst=dst, tb=tb: nc.vector.tensor_copy(
                    dst, bbank(tb).rearrange("p (c m) -> p c m", c=8)),
                 reads=[f"pb{tb}"], writes=[f"xnf{tc}"], tag=f"lnevac{tc}")

        for tc in range(NTC):
            ln_stats(tc)
            if tc >= 1:
                ln_apply(tc - 1)
        ln_apply(NTC - 1)

        wload(1, WK, f"ldwk_s{s}")


        def proj_feat(slot, dest, destname, evac_engine, tag):
            for mc in range(8):
                for nt in range(NNT):
                    b = (mc * NNT + nt) % 4
                    for kc in range(8):
                        P.op("tensor", lambda mc=mc, nt=nt, kc=kc, b=b, slot=slot:
                                nc.tensor.matmul(
                                    bank(b), wslot(slot)[:, kc * 1024 + mc * 128:kc * 1024 + (mc + 1) * 128],
                                    xnf[:, kc * 2048 + nt * 512:kc * 2048 + (nt + 1) * 512],
                                    start=(kc == 0), stop=(kc == 7)),
                             reads=[f"w{slot}c{kc}"] + [f"xnf{4 * nt + j}" for j in range(4)],
                             writes=[f"pb{b}"],
                             tag=f"{tag}mm{mc}_{nt}_{kc}")
                    if evac_engine == "scalar":
                        P.op("scalar", lambda mc=mc, nt=nt, b=b, dest=dest:
                                nc.scalar.activation(
                                    dest[:, mc * 2048 + nt * 512:mc * 2048 + (nt + 1) * 512],
                                    bank(b), AF.Copy),
                             reads=[f"pb{b}"], writes=[f"{destname}{mc}n{nt}"],
                             tag=f"{tag}ev{mc}_{nt}")
                    else:
                        P.op("vector", lambda mc=mc, nt=nt, b=b, dest=dest:
                                nc.vector.tensor_copy(
                                    dest[:, mc * 2048 + nt * 512:mc * 2048 + (nt + 1) * 512],
                                    bank(b)),
                             reads=[f"pb{b}"], writes=[f"{destname}{mc}n{nt}"],
                             tag=f"{tag}ev{mc}_{nt}")

        proj_feat(0, qctx, "q", "scalar", f"qp{s}")
        proj_feat(1, kreg, "k", "scalar", f"kp{s}")
        wload(0, WV, f"ldwv_s{s}")

        for tc in range(NTC):
            P.op("vector", lambda tc=tc: nc.vector.memset(
                    dataclasses.replace(vtok.ap(), offset=tc * 1056 + 64,
                                        ap=[[16896, 128], [VST, 16], [1, 1]]), 1.0),
                 writes=[f"vt{tc}"], tag=f"vones{tc}")
            for half in range(2):
                b = (tc * 2 + half) % 4
                for kc in range(8):
                    P.op("tensor", lambda tc=tc, half=half, kc=kc, b=b:
                            nc.tensor.matmul(
                                bank(b), xnf[:, kc * 2048 + tc * 128:kc * 2048 + (tc + 1) * 128],
                                wslot(0)[:, kc * 1024 + half * 512:kc * 1024 + (half + 1) * 512],
                                start=(kc == 0), stop=(kc == 7)),
                         reads=[f"w0c{kc}", f"xnf{tc}"], writes=[f"pb{b}"],
                         tag=f"vp{s}mm{tc}_{half}_{kc}")
                dst = dataclasses.replace(
                    vtok.ap(), offset=tc * 1056 + half * 8 * VST,
                    ap=[[16896, 128], [VST, 8], [1, 64]])
                P.op("vector", lambda dst=dst, b=b: nc.vector.tensor_copy(
                        dst, bank(b).rearrange("p (g x) -> p g x", g=8)),
                     reads=[f"pb{b}"], writes=[f"vt{tc}"], tag=f"vev{tc}_{half}")
        wload(1, WVF, f"ldwvf_s{s}")

        for mc in range(8):
            for nt in range(NNT):
                b = (mc * NNT + nt) % 4
                for kc in range(8):
                    P.op("tensor", lambda mc=mc, nt=nt, kc=kc, b=b: nc.tensor.matmul(
                            bank(b), wslot(1)[:, kc * 1024 + mc * 128:kc * 1024 + (mc + 1) * 128],
                            xnf[:, kc * 2048 + nt * 512:kc * 2048 + (nt + 1) * 512],
                            start=(kc == 0), stop=(kc == 7)),
                         reads=[f"w1c{kc}"] + [f"xnf{4 * nt + j}" for j in range(4)],
                         writes=[f"pb{b}"],
                         tag=f"vfmm{mc}_{nt}_{kc}")
                vs = vstage[:, (nt % 2) * 512:(nt % 2) * 512 + 512]
                P.op("vector", lambda mc=mc, b=b, vs=vs: nc.vector.tensor_scalar(
                        vs, bank(b), bv_s[:, mc:mc + 1], None, ALU.add),
                     reads=[f"pb{b}", "bv"], writes=[f"vstage{nt % 2}"],
                     tag=f"vfev{mc}_{nt}")
                P.op("sync", lambda mc=mc, nt=nt, vs=vs: nc.sync.dma_start(
                        vf_d[mc, :, nt * 512:(nt + 1) * 512], vs),
                     reads=[f"vstage{nt % 2}"], writes=[f"vfd{mc}n{nt}"], dma=True,
                     tag=f"vfst{mc}_{nt}", sem=f"vfst{nt % 2}")

        P.barrier(settle=settle_emit)
        for side, reg, regk in ((0, qctx, "q"), (1, kreg, "k")):
            for c in range(8):
                sq = g3[:, (c % 2) * 2048:(c % 2) * 2048 + 2048]
                P.op("vector", lambda c=c, reg=reg, sq=sq: nc.vector.scalar_tensor_tensor(
                        sq, reg[:, c * 2048:(c + 1) * 2048], 0.0,
                        reg[:, c * 2048:(c + 1) * 2048], ALU.add, ALU.mult),
                     reads=[f"{regk}{c}n{j}" for j in range(4)],
                     writes=[f"sq{c % 2}", f"xin{c % 2}"], tag=f"sq{side}_{c}")
                for nt in range(NNT):
                    qb = 4 + 2 * side + nt // 2
                    qbase = 64 * (nt % 2)
                    P.op("tensor", lambda c=c, nt=nt, sq=sq, qb=qb, qbase=qbase:
                            nc.tensor.matmul(
                                bank(qb, r0=qbase, r1=qbase + 16),
                                parexp[:, c * 16:(c + 1) * 16], sq[:, nt * 512:(nt + 1) * 512],
                                start=(c == 0 and nt % 2 == 0), stop=(c == 7),
                                tile_position=(0, qbase)),
                         reads=[f"sq{c % 2}"], writes=[f"pb{qb}"],
                         tag=f"q2mm{side}_{c}_{nt}")
            for nt in range(NNT):
                qb = 4 + 2 * side + nt // 2
                qbase = 64 * (nt % 2)
                P.op("vector", lambda nt=nt, side=side, qb=qb, qbase=qbase:
                        nc.vector.reduce_max(
                            m16[:, side * 8 + nt:side * 8 + nt + 1],
                            bank(qb, r0=qbase, r1=qbase + 16), AX.X),
                     reads=[f"pb{qb}"], writes=[f"m16_{side}"],
                     tag=f"q2red{side}_{nt}")

        vsbt_load(s, 0, f"vsbt{s}_0")
        vsbt_load(s, 1, f"vsbt{s}_1")
        P.op("vector", lambda: nc.vector.reduce_max(m16[0:16, 4:5], m16[0:16, 0:4], AX.X),
             reads=["m16_0"], writes=["mq2c"], tag="mq2c")
        P.op("vector", lambda: nc.vector.reduce_max(m16[0:16, 5:6], m16[0:16, 8:12], AX.X),
             reads=["m16_1"], writes=["mk2c"], tag="mk2c")
        P.op("vector", lambda: nc.vector.tensor_mul(m16[0:16, 6:7], m16[0:16, 4:5],
                                                    m16[0:16, 5:6]),
             reads=["mq2c", "mk2c"], writes=["prod2"], tag="prod2")
        P.op("scalar", lambda: nc.scalar.activation(m16[0:16, 7:8], m16[0:16, 6:7],
                                                    AF.Ln), reads=["prod2"],
             writes=["lnprod"], tag="lnprod")
        P.op("scalar", lambda: nc.scalar.activation(m16[0:16, 7:8], m16[0:16, 7:8],
                                                    AF.Exp, scale=kc_ph[0:16]),
             sreads=["lnprod"], reads=["konst"], writes=["lnprod"], tag="shifth")
        P.op("vector", lambda: nc.vector.tensor_scalar_mul(
                m16[0:16, 9:10], m16[0:16, 7:8], -1.0),
             reads=["lnprod"], writes=["nshcol"], tag="nshneg")
        P.op("sync", lambda: nc.sync.dma_start(nsh_d[s, :], m16[0:16, 9:10]),
             reads=["nshcol"], writes=[f"nshd{s}"], dma=True, tag=f"nshout{s}",
             sem="s_nsh")
        nshb = dataclasses.replace(nsh_d, offset=s * 16, ap=[[0, 128], [1, 16]])
        P.op("sync", lambda nshb=nshb: nc.sync.dma_start(negsh[:, :], nshb),
             reads=[f"nshd{s}"], writes=["negsh"], dma=True, tag=f"nshin{s}",
             sem="s_nsh")
        P.barrier(settle=settle_emit)

        def sc_quad(hp, qt, quad):
            h0, h1, hc = 2 * hp, 2 * hp + 1, hp
            q0 = qt * 512
            tb = quad % 2
            for kc4 in range(4):
                kc = quad * 4 + kc4
                for hi, h in enumerate((h0, h1)):
                    base = (h % 2) * 64
                    sb = hi * 3 + kc % 3
                    P.op("tensor", lambda kc=kc, base=base, sb=sb, q0=q0, hc=hc:
                            nc.tensor.matmul(
                                bank(sb),
                                kreg[base:base + 64,
                                     hc * 2048 + kc * 128:hc * 2048 + (kc + 1) * 128],
                                qctx[base:base + 64, hc * 2048 + q0:hc * 2048 + q0 + 512],
                                start=True, stop=True, tile_position=(base, 0)),
                         reads=[f"k{hc}n{kc // 4}", f"q{hc}n{qt}"],
                         writes=[f"pb{sb}"], tag=f"sc{h}_{qt}_{kc}")
                for hi, h in enumerate((h0, h1)):
                    sb = hi * 3 + kc % 3
                    texp = g3[:, (hi * 2 + tb) * 2048:(hi * 2 + tb) * 2048 + 2048]
                    P.op("scalar", lambda sb=sb, h=h, kc4=kc4, texp=texp:
                            nc.scalar.activation(
                                texp[:, kc4 * 512:(kc4 + 1) * 512], bank(sb),
                                AF.Exp, bias=negsh[:, h:h + 1]),
                         reads=[f"pb{sb}", "negsh"],
                         writes=[f"texp{hi}b{tb}k{kc4}"], tag=f"exp{h}_{qt}_{kc}")
            for kc4 in range(4):
                for hi, h in enumerate((h0, h1)):
                    texp = g3[:, (hi * 2 + tb) * 2048:(hi * 2 + tb) * 2048 + 2048]
                    slot = h % 2
                    kc = quad * 4 + kc4
                    off = 16384 + slot * T_VSBT + (1920 - kc * 128 + q0)
                    win = dataclasses.replace(g1.ap(), offset=off,
                                              ap=[[24576, 128], [1, 512]])
                    wt = g1[:, hi * 8192 + kc * 512:hi * 8192 + (kc + 1) * 512]
                    P.op("vector", lambda texp=texp, win=win, wt=wt, kc4=kc4:
                            nc.vector.tensor_tensor(
                                wt, texp[:, kc4 * 512:(kc4 + 1) * 512], win, ALU.mult),
                         reads=[f"texp{hi}b{tb}k{kc4}", f"vsbt{slot}"],
                         writes=[f"wt{hi}q{quad}"], tag=f"mult{h}_{qt}_{kc}")

        def ctx_group(hp, qt, quad):
            h0, h1 = 2 * hp, 2 * hp + 1
            for hi, h in enumerate((h0, h1)):
                cb = 6 + hi
                for qc in range(4):
                    for kc in range(quad * 4, quad * 4 + 4):
                        P.op("tensor", lambda hi=hi, h=h, qc=qc, kc=kc, cb=cb:
                                nc.tensor.matmul(
                                    bank(cb, c0=qc * 65, w=65),
                                    g1[:, hi * 8192 + kc * 512 + qc * 128:
                                       hi * 8192 + kc * 512 + (qc + 1) * 128],
                                    vtok[:, kc * 1056 + h * VST:kc * 1056 + h * VST + 65],
                                    start=(qc == 0 and kc == 0), stop=(kc == 15)),
                             reads=[f"wt{hi}q{kc // 4}", f"vt{kc}"],
                             writes=[f"pb{cb}"], tag=f"ctx{h}_{qt}_{qc}_{kc}")

        def ctx_evac(hp, qt):
            h0, h1, hc = 2 * hp, 2 * hp + 1, hp
            for hi, h in enumerate((h0, h1)):
                cb = 6 + hi
                zap = dataclasses.replace(psap, offset=cb * 512 + 64,
                                          ap=[[4096, 128], [65, 4], [1, 1]])
                P.op("vector", lambda zap=zap, hi=hi: nc.vector.reciprocal(
                        zc[:, hi * 4:hi * 4 + 4], zap),
                     reads=[f"pb{cb}"], writes=[f"zc{hi}"], tag=f"zrec{h}_{qt}")
                cst = ctxst[:, hi * 260:hi * 260 + 260]
                for qc in range(4):
                    P.op("vector", lambda qc=qc, cb=cb, cst=cst, hi=hi:
                            nc.vector.tensor_scalar(
                                cst[:, qc * 64:(qc + 1) * 64],
                                bank(cb, c0=qc * 65, w=64),
                                zc[:, hi * 4 + qc:hi * 4 + qc + 1], None, ALU.mult),
                         reads=[f"pb{cb}"], sreads=[f"zc{hi}"],
                         writes=[f"cst{hi}"], tag=f"ctxsc{h}_{qt}_{qc}")
                tb = cb
                for qc in range(4):
                    P.op("tensor", lambda qc=qc, cst=cst, tb=tb: nc.tensor.transpose(
                            bbank(tb, r1=64, c0=qc * 128, w=128),
                            cst[:, qc * 64:(qc + 1) * 64], identb[:, :]),
                         reads=[f"cst{hi}", "identb"], writes=[f"pb{tb}"],
                         tag=f"ctxtr{h}_{qt}_{qc}")
                base = (h % 2) * 64
                P.op("vector", lambda tb=tb, base=base, qt=qt, hc=hc: nc.vector.tensor_copy(
                        qctx[base:base + 64, hc * 2048 + qt * 512:hc * 2048 + (qt + 1) * 512],
                        bbank(tb, r1=64, w=512)[0:64, 0:512]),
                     reads=[f"pb{tb}"], writes=[f"q{hc}n{qt}"],
                     tag=f"ctxev{h}_{qt}")

        tiles = [(hp, qt) for hp in range(8) for qt in range(4)]
        for quad in range(4):
            sc_quad(0, 0, quad)
        for ti, (hp, qt) in enumerate(tiles):
            nxt = tiles[ti + 1] if ti + 1 < len(tiles) else None
            for quad in range(4):
                ctx_group(hp, qt, quad)
                if nxt is not None:
                    sc_quad(nxt[0], nxt[1], quad)
            ctx_evac(hp, qt)
            if qt == 2:
                if hp < 7:
                    vsbt_load(s, 2 * hp + 2, f"vsbt{s}_{2 * hp + 2}")
                    vsbt_load(s, 2 * hp + 3, f"vsbt{s}_{2 * hp + 3}")
                elif s == 0:
                    vsbt_load(1, 0, "vsbt1_0")
                    vsbt_load(1, 1, "vsbt1_1")
        P.barrier(settle=settle_emit)

        for mc in range(8):
            P.op("sync", lambda mc=mc: nc.sync.dma_start(
                    kreg[:, mc * 2048:(mc + 1) * 2048], vf_d[mc, :, :]),
                 reads=[f"vfd{mc}n{j}" for j in range(4)]
                       + ([f"vf{mc - 2}"] if mc >= 2 else []),
                 writes=[f"vf{mc}"], dma=True, tag=f"ldvf{mc}", sem=f"vfld{mc % 2}")
        for c in range(8):
            P.op("gpsimd", lambda c=c: nc.gpsimd.dma_start(
                    g3[:, c * 256:(c + 1) * 256], w1t_d[c * 128:(c + 1) * 128, :]),
                 writes=[f"w1t{c}", "w1tCH"], dma=True, tag=f"ldw1t{c}", sem="w1t")
        for c in range(2):
            P.op("gpsimd", lambda c=c: nc.gpsimd.dma_start(
                    g3[:, 2048 + c * 1024:2048 + (c + 1) * 1024],
                    w2t_d[c * 128:(c + 1) * 128, :]),
                 writes=[f"w2t{c}", "w2tCH"], dma=True, tag=f"ldw2t{c}", sem="w2t")
        for c in range(8):
            P.op("gpsimd", lambda c=c: nc.gpsimd.dma_start(
                    g3[:, 4096 + c * 72:4096 + (c + 1) * 72],
                    wgp_d[c * 128:(c + 1) * 128, :]),
                 writes=[f"wgp{c}", "wgpCH"], dma=True, tag=f"ldwgp{c}", sem="wgp")
        P.op("gpsimd", lambda: nc.gpsimd.dma_start(g3[0:8, 4672:4672 + 1024], b2m_d[:, :]),
             writes=["b2m"], dma=True, tag="ldb2m", sem="s_b2m")
        wload(1, WO, f"ldwo_s{s}")
        weT = g3[0:8, 5696:5696 + 2048]

        class _Rec:
            def __init__(self, lst):
                self.lst = lst
            def op(self, *a, **k):
                self.lst.append((a, k))

        for blk in range(2):
            for tcl in range(8):
                tc = blk * 8 + tcl
                gb = 4 + (tc % 2)
                for kc in range(8):
                    P.op("tensor", lambda tc=tc, kc=kc, gb=gb: nc.tensor.matmul(
                            bank(gb, w=72), kreg[:, kc * 2048 + tc * 128:kc * 2048 + (tc + 1) * 128],
                            g3[:, 4096 + kc * 72:4096 + (kc + 1) * 72],
                            start=(kc == 0), stop=(kc == 7)),
                         reads=[f"vf{kc}", f"wgp{kc}"], writes=[f"pb{gb}"],
                         tag=f"gmm{tc}_{kc}")
                gp_ = tcl * 128
                gev = gwork[:, gp_:gp_ + 72]
                P.op("vector", lambda gb=gb, gev=gev: nc.vector.tensor_copy(gev, bank(gb, w=72)),
                     reads=[f"pb{gb}"], writes=[f"g_ev{tcl}"], tag=f"gev{tc}")
            GB = [[] for _ in range(8)]
            for tcl in range(8):
                tc = blk * 8 + tcl
                PT = _Rec(GB[tcl])
                gp_ = tcl * 128
                gev = gwork[:, gp_:gp_ + 72]
                r_ = gwork[:, gp_ + 72:gp_ + 80]
                PT.op("vector", lambda gev=gev, r_=r_: nc.vector.tensor_tensor(
                        r_, gev[:, 0:8], bgb[:, :], ALU.add),
                      reads=[f"g_ev{tcl}", "bgb"], writes=[f"g_r{tcl}"], tag=f"gr{tc}")
                pv2 = gev[:, 8:72]
                PT.op("vector", lambda pv2=pv2: nc.vector.scalar_tensor_tensor(
                        pv2, pv2, 0.0, pv2, ALU.add, ALU.mult),
                      reads=[f"g_ev{tcl}"], writes=[f"g_ev{tcl}"], tag=f"gpv2{tc}")
                c2 = gwork[:, gp_ + 80:gp_ + 88]
                PT.op("vector", lambda pv2=pv2, c2=c2: nc.vector.tensor_add(
                        c2, pv2[:, 0:8], pv2[:, 8:16]), reads=[f"g_ev{tcl}"],
                      writes=[f"g_c2{tcl}"])
                for pstep in range(2, 8):
                    PT.op("vector", lambda pv2=pv2, c2=c2, pstep=pstep: nc.vector.tensor_add(
                            c2, c2, pv2[:, pstep * 8:(pstep + 1) * 8]),
                          reads=[f"g_ev{tcl}", f"g_c2{tcl}"], writes=[f"g_c2{tcl}"])

                def gln(srck, src, g, gk, b, bk, outk, out):
                    glb = 10 + tcl * 3
                    mean = lnw[:, glb:glb + 1]; ss2 = lnw[:, glb + 1:glb + 2]
                    rst = lnw[:, glb + 2:glb + 3]
                    rc = gwork[:, gp_:gp_ + 8] if outk.startswith("g_lnr") else gwork[:, gp_ + 16:gp_ + 24]
                    scr = gwork[:, gp_ + 8:gp_ + 16]
                    PT.op("vector", lambda src=src, mean=mean: nc.vector.reduce_sum(mean, src, AX.X),
                          reads=[srck], writes=[f"g_mean{tcl}"])
                    PT.op("vector", lambda mean=mean: nc.vector.tensor_scalar_mul(mean, mean, 0.125),
                          reads=[f"g_mean{tcl}"], writes=[f"g_mean{tcl}"])
                    PT.op("vector", lambda src=src, rc=rc, mean=mean: nc.vector.tensor_scalar(
                            rc, src, mean, None, ALU.subtract),
                          reads=[srck], sreads=[f"g_mean{tcl}"], writes=[outk + "_rc", f"g_ev{tcl}"])
                    PT.op("vector", lambda rc=rc, scr=scr: nc.vector.scalar_tensor_tensor(
                            scr, rc, 0.0, rc, ALU.add, ALU.mult),
                          reads=[outk + "_rc"], writes=[f"g_scr{tcl}", f"g_ev{tcl}"])
                    PT.op("vector", lambda scr=scr, ss2=ss2: nc.vector.reduce_sum(ss2, scr, AX.X),
                          reads=[f"g_scr{tcl}"], writes=[f"g_ss2{tcl}"])
                    PT.op("scalar", lambda ss2=ss2, rst=rst: nc.scalar.activation(
                            rst, ss2, AF.Ln, bias=kc_eps, scale=kc_8),
                          reads=[f"g_ss2{tcl}", "konst"], writes=[f"g_rst{tcl}"])
                    PT.op("scalar", lambda rst=rst: nc.scalar.activation(
                            rst, rst, AF.Exp, scale=kc_nh),
                          sreads=[f"g_rst{tcl}"], reads=["konst"], writes=[f"g_rst{tcl}"])
                    PT.op("vector", lambda rc=rc, rst=rst, out=out: nc.vector.tensor_scalar(
                            out, rc, rst, None, ALU.mult),
                          reads=[outk + "_rc"], sreads=[f"g_rst{tcl}"], writes=[outk])
                    PT.op("vector", lambda out=out, g=g: nc.vector.tensor_mul(out, out, g[:, :]),
                          reads=[outk, gk], writes=[outk])
                    PT.op("vector", lambda out=out, b=b: nc.vector.tensor_add(out, out, b[:, :]),
                          reads=[outk, bk], writes=[outk])

                lnr = gwork[:, gp_ + 96:gp_ + 104]; lnc = gwork[:, gp_ + 104:gp_ + 112]
                gln(f"g_r{tcl}", r_, grb, "grb", brb, "brb", f"g_lnr{tcl}", lnr)
                cval = gwork[:, gp_ + 88:gp_ + 96]
                PT.op("scalar", lambda c2=c2, cval=cval: nc.scalar.activation(
                        cval, c2, AF.Ln, scale=kc_8),
                      reads=[f"g_c2{tcl}", "konst"], writes=[f"g_cval{tcl}"], tag=f"gcln{tc}")
                PT.op("scalar", lambda cval=cval: nc.scalar.activation(
                        cval, cval, AF.Exp, scale=kc_ph),
                      sreads=[f"g_cval{tcl}"], reads=["konst"], writes=[f"g_cval{tcl}"],
                      tag=f"gcexp{tc}")
                gln(f"g_cval{tcl}", cval, gcb, "gcb", bcb, "bcb", f"g_lnc{tcl}", lnc)
                sgs = sgat[:, tcl * 8:(tcl + 1) * 8]
                PT.op("vector", lambda lnr=lnr, lnc=lnc, sgs=sgs: nc.vector.scalar_tensor_tensor(
                        sgs, lnc, sigb[:, :], lnr, ALU.mult, ALU.add),
                      reads=[f"g_lnr{tcl}", f"g_lnc{tcl}"], sreads=["sigb"], writes=[f"g_s{tcl}"],
                      tag=f"gcomb{tc}")
                mx8 = gwork[:, gp_ + 24:gp_ + 32]; mxp = gwork[:, gp_ + 32:gp_ + 40]
                PT.op("vector", lambda mx8=mx8, sgs=sgs: nc.vector.max(out=mx8, in_=sgs),
                      reads=[f"g_s{tcl}"], writes=[f"g_mx8{tcl}", f"g_ev{tcl}"], tag=f"gmax{tc}")
                PT.op("vector", lambda mxp=mxp: nc.vector.memset(mxp, 1e38),
                      writes=[f"g_mxp{tcl}", f"g_ev{tcl}"])
                PT.op("vector", lambda mxp=mxp, mx8=mx8: nc.vector.tensor_copy(mxp[:, 0:1], mx8[:, 0:1]),
                      reads=[f"g_mx8{tcl}"], writes=[f"g_mxp{tcl}"])
                srep = gwork[:, gp_ + 40:gp_ + 48]
                PT.op("vector", lambda srep=srep, mxp=mxp, sgs=sgs: nc.vector.match_replace(
                        out=srep, in_to_replace=mxp, in_values=sgs, imm_value=-3e38),
                      reads=[f"g_mxp{tcl}", f"g_s{tcl}"], writes=[f"g_srep{tcl}", f"g_ev{tcl}"],
                      tag=f"gmr1{tc}")
                PT.op("vector", lambda srep=srep, tc=tc: nc.vector.tensor_single_scalar(
                        eq1a[:, tc * 8:(tc + 1) * 8], srep, -1e38, ALU.is_le),
                      reads=[f"g_srep{tcl}"], writes=[f"eq1_{tc}"], tag=f"geq1{tc}")
                PT.op("vector", lambda mxp=mxp: nc.vector.memset(mxp, 1e38),
                      reads=[f"g_srep{tcl}"], writes=[f"g_mxp{tcl}"])
                PT.op("vector", lambda mxp=mxp, mx8=mx8: nc.vector.tensor_copy(mxp[:, 0:1], mx8[:, 1:2]),
                      reads=[f"g_mx8{tcl}"], writes=[f"g_mxp{tcl}"])
                srep2 = gwork[:, gp_ + 48:gp_ + 56]
                PT.op("vector", lambda srep=srep, srep2=srep2, mxp=mxp: nc.vector.match_replace(
                        out=srep2, in_to_replace=mxp, in_values=srep, imm_value=-3e38),
                      reads=[f"g_mxp{tcl}", f"g_srep{tcl}"], writes=[f"g_srep2{tcl}", f"g_ev{tcl}"],
                      tag=f"gmr2{tc}")
                PT.op("vector", lambda srep2=srep2, tc=tc: nc.vector.tensor_single_scalar(
                        eq2a[:, tc * 8:(tc + 1) * 8], srep2, -1e38, ALU.is_le),
                      reads=[f"g_srep2{tcl}"], writes=[f"eq2_{tc}"], tag=f"geq2{tc}")
                PT.op("vector", lambda tc=tc: nc.vector.tensor_sub(
                        eq2a[:, tc * 8:(tc + 1) * 8], eq2a[:, tc * 8:(tc + 1) * 8],
                        eq1a[:, tc * 8:(tc + 1) * 8]),
                      reads=[f"eq1_{tc}", f"eq2_{tc}"], writes=[f"eq2_{tc}"])
                PT.op("vector", lambda mx8=mx8, tc=tc: nc.vector.tensor_sub(
                        dda[:, tc:tc + 1], mx8[:, 0:1], mx8[:, 1:2]),
                      reads=[f"g_mx8{tcl}"], writes=[f"dd_{tc}"], tag=f"gdd{tc}")
            mx = max(len(g_) for g_ in GB)
            for st in range(mx):
                for tcl in range(8):
                    if st < len(GB[tcl]):
                        a, k = GB[tcl][st]
                        P.op(*a, **k)
        P.op("scalar", lambda: nc.scalar.activation(w1a[:, :], dda[:, :], AF.Sigmoid),
             reads=[f"dd_{tc}" for tc in range(NTC)], writes=["w1a"], tag="gsig1")
        P.op("scalar", lambda: nc.scalar.activation(w2a[:, :], dda[:, :], AF.Sigmoid,
                                                    scale=kc_m1),
             reads=[f"dd_{tc}" for tc in range(NTC)] + ["konst"], writes=["w2a"],
             tag="gsig2")
        P.op("scalar", lambda: nc.scalar.drain(), reads=["w1a", "w2a"], writes=["wdrain"],
             tag="wsigdrain")
        for tc in range(NTC):
            gp_ = (tc % 8) * 128
            t1 = gwork[:, gp_ + 8:gp_ + 16]; t2 = gwork[:, gp_ + 16:gp_ + 24]
            P.op("vector", lambda tc=tc, t1=t1: nc.vector.tensor_scalar(
                    t1, eq1a[:, tc * 8:(tc + 1) * 8], w1a[:, tc:tc + 1], None, ALU.mult),
                 reads=[f"eq1_{tc}", "wdrain"], sreads=["w1a"], writes=[f"g_t1{tc % 8}"],
                 tag=f"gwe1{tc}")
            P.op("vector", lambda tc=tc, t2=t2: nc.vector.tensor_scalar(
                    t2, eq2a[:, tc * 8:(tc + 1) * 8], w2a[:, tc:tc + 1], None, ALU.mult),
                 reads=[f"eq2_{tc}", "wdrain"], sreads=["w2a"], writes=[f"g_t2{tc % 8}"],
                 tag=f"gwe2{tc}")
            web16 = web_s[:, (tc % 8) * 8:(tc % 8) * 8 + 8]
            P.op("vector", lambda t1=t1, t2=t2, web16=web16: nc.vector.tensor_tensor(
                    web16, t1, t2, ALU.add),
                 reads=[f"g_t1{tc % 8}", f"g_t2{tc % 8}"], writes=[f"g_web{tc % 8}"], tag=f"gwe{tc}")
            tb = 6 + (tc % 2)
            P.op("tensor", lambda web16=web16, tb=tb: nc.tensor.transpose(
                    bbank(tb, r1=8, w=128)[0:8, 0:128], web16, identb[:, :]),
                 reads=[f"g_web{tc % 8}", "identb"], writes=[f"pb{tb}"], tag=f"gwtr{tc}")
            P.op("vector", lambda tc=tc, tb=tb: nc.vector.tensor_copy(
                    weT[:, tc * 128:(tc + 1) * 128], bbank(tb, r1=8, w=128)[0:8, 0:128]),
                 reads=[f"pb{tb}"], writes=[f"weT{tc // 4}"], tag=f"gwev{tc}")

        P.op("vector", lambda: nc.vector.drain(),
             reads=[f"weT{j}" for j in range(4)], writes=["weTdrain"], tag="wetdrain")
        for hc2 in range(2):
            for nt in range(NNT):
                hb = (hc2 * NNT + nt) % 2
                wb = 2 + hb
                P.op("tensor", lambda hc2=hc2, nt=nt, wb=wb: nc.tensor.matmul(
                        bank(wb), expand[:, hc2 * 128:(hc2 + 1) * 128],
                        weT[:, nt * 512:(nt + 1) * 512], start=True, stop=True),
                     reads=[f"weT{nt}", "expand", "weTdrain"], writes=[f"pb{wb}"],
                     tag=f"wex{hc2}_{nt}")
                for kc in range(8):
                    P.op("tensor", lambda hc2=hc2, nt=nt, kc=kc, hb=hb: nc.tensor.matmul(
                            bank(hb), g3[:, kc * 256 + hc2 * 128:kc * 256 + (hc2 + 1) * 128],
                            kreg[:, kc * 2048 + nt * 512:kc * 2048 + (nt + 1) * 512],
                            start=(kc == 0), stop=(kc == 7)),
                         reads=[f"w1t{kc}", f"vf{kc}"], writes=[f"pb{hb}"],
                         tag=f"hmm{hc2}_{nt}_{kc}")
                sil = silst[:, hb * 512:(hb + 1) * 512]
                P.op("scalar", lambda hb=hb, hc2=hc2, sil=sil: nc.scalar.activation(
                        sil, bank(hb), AF.Silu, bias=b1_s[:, hc2:hc2 + 1]),
                     reads=[f"pb{hb}", "b1"], writes=[f"sil{hb}"],
                     tag=f"silu{hc2}_{nt}")
                P.op("vector", lambda hc2=hc2, nt=nt, wb=wb, sil=sil: nc.vector.tensor_tensor(
                        g1[:, hc2 * 2048 + nt * 512:hc2 * 2048 + (nt + 1) * 512],
                        bank(wb), sil, ALU.mult),
                     reads=[f"pb{wb}", f"sil{hb}"], writes=[f"hs{hc2}n{nt}"],
                     tag=f"hscale{hc2}_{nt}")
        for mc in range(8):
            for nt in range(NNT):
                db = (mc * NNT + nt) % 2
                P.op("tensor", lambda mc=mc, nt=nt, db=db: nc.tensor.matmul(
                        bank(db), g3[0:8, 4672 + mc * 128:4672 + (mc + 1) * 128],
                        weT[:, nt * 512:(nt + 1) * 512], start=True, stop=False),
                     reads=["b2m", f"weT{nt}"], writes=[f"pb{db}"],
                     tag=f"db2{mc}_{nt}")
                for hc2 in range(2):
                    P.op("tensor", lambda mc=mc, nt=nt, hc2=hc2, db=db: nc.tensor.matmul(
                            bank(db), g3[:, 2048 + hc2 * 1024 + mc * 128:2048 + hc2 * 1024 + (mc + 1) * 128],
                            g1[:, hc2 * 2048 + nt * 512:hc2 * 2048 + (nt + 1) * 512],
                            start=False, stop=(hc2 == 1)),
                         reads=[f"w2t{hc2}", f"hs{hc2}n{nt}"], writes=[f"pb{db}"],
                         tag=f"dmm{mc}_{nt}_{hc2}")
                P.op("vector", lambda mc=mc, nt=nt, db=db: nc.vector.tensor_tensor(
                        xnf[:, mc * 2048 + nt * 512:mc * 2048 + (nt + 1) * 512],
                        bank(db), kreg[:, mc * 2048 + nt * 512:mc * 2048 + (nt + 1) * 512],
                        ALU.add),
                     reads=[f"pb{db}", f"vf{mc}"], writes=[f"ffn{mc}n{nt}"],
                     tag=f"dev{mc}_{nt}")
        P.barrier(settle=settle_emit)

        for tc in range(NTC):
            xre = kf[:, (tc % 2) * 1024:(tc % 2) * 1024 + 1024]
            P.op("sync", lambda s=s, tc=tc, xre=xre: nc.sync.dma_start(
                    xre, x_d[s, tc * 128:(tc + 1) * 128, :]),
                 writes=[f"xre{tc % 2}"], dma=True, tag=f"ldxre{s}_{tc}", sem=f"xre{tc % 2}")
            tb = 6 + (tc % 2)
            for mc in range(8):
                P.op("tensor", lambda mc=mc, tc=tc, tb=tb: nc.tensor.transpose(
                        bbank(tb, c0=mc * 128, w=128),
                        xnf[:, mc * 2048 + tc * 128:mc * 2048 + (tc + 1) * 128],
                        identb[:, :]),
                     reads=[f"ffn{mc}n{tc // 4}", "identb"], writes=[f"pb{tb}"],
                     tag=f"ftr{tc}_{mc}")
            for nt2 in range(2):
                ab = (tc * 2 + nt2) % 4
                for kc in range(8):
                    P.op("tensor", lambda tc=tc, nt2=nt2, kc=kc, ab=ab: nc.tensor.matmul(
                            bank(ab), qctx[:, kc * 2048 + tc * 128:kc * 2048 + (tc + 1) * 128],
                            wslot(1)[:, kc * 1024 + nt2 * 512:kc * 1024 + (nt2 + 1) * 512],
                            start=(kc == 0), stop=(kc == 7)),
                         reads=[f"w1c{kc}", f"q{kc}n{tc // 4}"], writes=[f"pb{ab}"],
                         tag=f"womm{tc}_{nt2}_{kc}")
                ob = kf[:, 2048 + (tc % 2) * 1024 + nt2 * 512:2048 + (tc % 2) * 1024 + (nt2 + 1) * 512]
                P.op("vector", lambda ob=ob, ab=ab, xre=xre, nt2=nt2: nc.vector.tensor_tensor(
                        ob, bank(ab), xre[:, nt2 * 512:(nt2 + 1) * 512], ALU.add),
                     reads=[f"pb{ab}", f"xre{tc % 2}"], writes=[f"ob{tc % 2}_{nt2}"],
                     tag=f"asm1_{tc}_{nt2}")
                P.op("vector", lambda ob=ob, tb=tb, nt2=nt2: nc.vector.tensor_tensor(
                        ob, bbank(tb, c0=nt2 * 512, w=512), ob, ALU.add),
                     reads=[f"pb{tb}", f"ob{tc % 2}_{nt2}"], writes=[f"ob{tc % 2}_{nt2}"],
                     tag=f"asm2_{tc}_{nt2}")
            P.op("sync", lambda s=s, tc=tc: nc.sync.dma_start(
                    out_d[s, tc * 128:(tc + 1) * 128, :],
                    kf[:, 2048 + (tc % 2) * 1024:2048 + (tc % 2) * 1024 + 1024]),
                 reads=[f"ob{tc % 2}_0", f"ob{tc % 2}_1"], dma=True,
                 tag=f"stout{s}_{tc}", sem=f"outst{tc % 2}")
        if s == 0:
            wload(0, WQ, "ldwq_s1")
        P.barrier(settle=settle_emit)

    P.emit()
    return nc


def kernel(**inputs):
    x = np.ascontiguousarray(inputs["L0_attn_hidden_states"])
    wq = np.ascontiguousarray(inputs["L0_attn_Wq"])
    wk = np.ascontiguousarray(inputs["L0_attn_Wk"])
    wv = np.ascontiguousarray(inputs["L0_attn_Wv"])
    wo = np.ascontiguousarray(inputs["L0_attn_Wo"])
    relb = np.ascontiguousarray(inputs["L0_attn_rel_bias"])
    wvf = np.ascontiguousarray(inputs["L0_ffn_Wv"].T)
    bv = np.ascontiguousarray(inputs["L0_ffn_bv"])
    wg = inputs["L0_ffn_Wg"]
    bg = inputs["L0_ffn_bg"]
    wp = inputs["L0_ffn_Wp"]
    alpha = np.asarray(inputs["L0_ffn_alpha"]).reshape(())
    g_r = inputs["L0_ffn_g_r"]; b_r = inputs["L0_ffn_b_r"]
    g_c = inputs["L0_ffn_g_c"]; b_c = inputs["L0_ffn_b_c"]
    w1 = inputs["L0_ffn_W1"]
    b1 = inputs["L0_ffn_b1"]
    w2 = inputs["L0_ffn_W2"]
    b2 = np.ascontiguousarray(inputs["L0_ffn_b2"])

    wgp = np.concatenate([np.asarray(wg).T,
                          np.asarray(wp).transpose(2, 1, 0).reshape(1024, 64)],
                         axis=1).astype(np.float32)
    w1t = np.asarray(w1).transpose(2, 0, 1).reshape(1024, 256).astype(np.float32)
    w2t = np.asarray(w2).transpose(0, 2, 1).reshape(256, 1024).astype(np.float32)
    bvr = np.asarray(bv).reshape(8, 128).T.astype(np.float32)
    b1r = np.asarray(b1).reshape(256).reshape(2, 128).T.astype(np.float32)
    gp = np.zeros((1, 48), np.float32)
    gp[0, 0:8] = g_r; gp[0, 8:16] = b_r; gp[0, 16:24] = g_c; gp[0, 24:32] = b_c
    gp[0, 32:40] = bg; gp[0, 40] = alpha

    buckets = _bucket_table()
    t_idx = np.arange(4095)
    bt = buckets[(2047 - t_idx) + 2047]
    import ml_dtypes
    ohrev = np.zeros((32, 4095), dtype=ml_dtypes.bfloat16)
    ohrev[bt, t_idx] = 1
    identb = np.eye(128, dtype=ml_dtypes.bfloat16)
    identf = np.eye(128, dtype=np.float32)
    onesr = np.ones((1, 128), np.float32)
    expand = np.zeros((8, 256), dtype=ml_dtypes.bfloat16)
    for e in range(8):
        expand[e, e * 32:(e + 1) * 32] = 1
    parexp = np.zeros((128, 128), dtype=ml_dtypes.bfloat16)
    for c in range(8):
        for p in range(128):
            parexp[p, c * 16 + 2 * c + (1 if p >= 64 else 0)] = 1

    nc = build_program()

    common = dict(wq=wq, wk=wk, wv=wv, wo=wo, wvf=np.ascontiguousarray(wvf),
                  wgp=wgp, w1t=w1t, w2t=w2t, b2m=b2, bvr=bvr, b1r=b1r, gp=gp,
                  relb=relb, ohrev=ohrev, identb=identb, identf=identf,
                  onesr=onesr, expand=np.asarray(expand), parexp=np.asarray(parexp))
    in_maps = []
    for core in range(8):
        m = dict(common)
        m["x"] = np.ascontiguousarray(x[core * 2:(core + 1) * 2])
        in_maps.append(m)
    res = run_bass_kernel_spmd(nc, in_maps, core_ids=list(range(8)))
    kernel.last_results = res
    return np.concatenate([r["out"] for r in res.results], axis=0)
